# version 17
# speedup vs baseline: 1.4397x; 1.0025x over previous
import numpy as np
import ml_dtypes
from contextlib import ExitStack

import concourse.bass as bass
import concourse.mybir as mybir
from concourse.bass_utils import run_bass_kernel_spmd
from concourse.bass_core_rust import IndirectOffsetOnAxis

F32 = mybir.dt.float32
BF16 = mybir.dt.bfloat16
AF = mybir.ActivationFunctionType
ALU = mybir.AluOpType
AX = mybir.AxisListType
BF = ml_dtypes.bfloat16

NCORES = 8
DM = 2048
NTOK = 2048
SEQ = 8192
HFF = 5632
NHT = HFF // 128
NDT = DM // 128
QC = 512
EPS_LN = 1e-5
EPS_Q = 1e-8


class Prog:
    ENGS = ("sync", "tensor", "scalar", "vector", "gpsimd")

    def __init__(self, nc):
        self.nc = nc
        self.q = {e: [] for e in self.ENGS}
        self.chains = {}

    def top(self, eng, name, *args, wait=(), inc=(), **kw):
        ch = self.chains[eng]
        ch.n += 1
        n = ch.n
        wait = tuple(wait)
        inc = tuple(inc)

        def f(E):
            for sem, v in wait:
                E.wait_ge(sem, v)
            getattr(E, name)(*args, **kw).then_inc(ch.h, 1)
            E.wait_ge(ch.h, n)
            for sem, k in inc:
                E.sem_inc(sem, k)
        self.q[eng].append(f)

    def op(self, eng, name, *args, wait=(), inc=(), **kw):
        wait = tuple(wait)
        inc = tuple(inc)

        def f(E):
            for s, v in wait:
                E.wait_ge(s, v)
            try:
                ins = getattr(E, name)(*args, **kw)
            except Exception as e:
                raise RuntimeError(f"op {eng}.{name} args={[type(a).__name__ for a in args]}") from e
            for s, n in inc:
                ins.then_inc(s, n)
        self.q[eng].append(f)

    def wait(self, eng, sem, val):
        self.q[eng].append(lambda E: E.wait_ge(sem, val))

    def finish(self):
        with self.nc.Block() as block:
            for e in self.ENGS:
                if self.q[e]:
                    getattr(block, e)(lambda E, fns=tuple(self.q[e]): [f(E) for f in fns])


class Ctr:

    def __init__(self, stack, nc, name):
        self.h = stack.enter_context(nc.semaphore(name))
        self.n = 0

    def inc(self, k=1):
        self.n += k
        return (self.h, k)

    def at(self, n=None):
        return (self.h, self.n if n is None else n)



class Rel:

    def __init__(self):
        self.d = {}

    def use(self, key, eng, ctr, n=None):
        self.d.setdefault(key, []).append((eng, ctr.h, ctr.n if n is None else n))

    def claim(self, key, producer_eng):
        edges = self.d.pop(key, [])
        out = []
        for eng, h, n in edges:
            if eng != producer_eng:
                out.append((h, n))
        return out


def emit_ffn(nc, p, c, t, rT_in, w1L, w2L, w3L, out_hbm, out_f32, pre=None):
    PE, ACs, DV = c["PE"], c["AC"], c["DV"]
    QE, QC = "vector", DV
    L = c["lanes"]
    rel = Rel()
    one_col, one_row = t["one_col"], t["one_row"]
    gpart = t["gpart"]
    S, rbf, hid = t["S"], t["rbf"], t["hid"]
    fbuf = [S[:, 0:2048], S[:, 2048:4096]]
    w3f = [S[:, 4096:4096 + 1408], S[:, 5504:5504 + 1408]]
    wq3 = [hid[:, 0:1408], hid[:, 1408:2816]]
    w3qb = [rbf[:, 0:NHT * 128], rbf[:, NHT * 128:2 * NHT * 128]]
    qa = t["qa"]
    wq, sg = t["wq"], t["sg"]
    ps_a, ps_b = [t["ps_a0"], t["ps_a1"]], [t["ps_b0"], t["ps_b1"]]
    ps_c = [t["ps_c0"], t["ps_c1"]]
    ps_s1, ps_s2 = t["ps_s1"], t["ps_s2"]
    w1q_s, w2q_s, w3q_s = t["w1q_s"], t["w2q_s"], t["w3q_s"]
    W3B = 1408
    misc_ready = L["MISC"].at()

    if pre is not None:
        thr_b, nthr_b = pre["thr_b"], pre["nthr_b"]
        thr_ready = pre["ready"]
        w3q_done = list(pre["w3q_done"])
    else:
        ncol = 0
        for m in range(3):
            blocks = []
            if m < 2:
                src = (w1L, w2L)[m]
                for blk in range(NHT):
                    blocks.append((src[blk], fbuf, "LF"))
            else:
                for dt in range(NDT):
                    for qq in range(4):
                        blocks.append((w3L[dt][:, qq * W3B:(qq + 1) * W3B], w3f, "LW3"))
            for j, (srcap, bufs, lane) in enumerate(blocks):
                slot = j % 2
                ln = L[f"{lane}{slot}"]
                p.op("sync", "dma_start", bufs[slot], srcap,
                     wait=rel.claim((lane, slot), "sync"), inc=[ln.inc(16)])
                em = p.top if ncol == 151 else p.op
                em("vector", "tensor_reduce", gpart[:, ncol:ncol + 1], bufs[slot],
                   AX.X, ALU.add, apply_absolute_value=True,
                   wait=[ln.at()], inc=[DV.inc()])
                rel.use((lane, slot), "vector", DV)
                ncol += 1
        p.op("tensor", "matmul", ps_s1[:, 0:ncol], one_col[:, :], gpart[:, 0:ncol],
             start=True, stop=True, wait=[DV.at(), misc_ready], inc=[PE.inc()])
        sgam = t["row_a"]
        p.op("scalar", "activation", sgam[0:1, 0:ncol], ps_s1[:, 0:ncol], AF.Copy,
             wait=[PE.at()], inc=[ACs.inc()])
        rel.use(("ps", "s1"), "scalar", ACs)
        gsum, thr = t["gsum"], t["thr"]
        p.wait("vector", *ACs.at())
        spans = [(0, 44), (44, 88), (88, 152)]
        for m, (c0, c1) in enumerate(spans):
            p.top("vector", "tensor_reduce", gsum[:, m:m + 1], sgam[0:1, c0:c1],
                  AX.X, ALU.add)
            p.top("vector", "tensor_scalar", thr[:, m:m + 1], gsum[:, m:m + 1],
                  0.5 / (DM * HFF), 0.5 * EPS_Q, ALU.mult, ALU.add,
                  inc=[DV.inc()] if m == 2 else [])
        p.op("tensor", "matmul", ps_c[0][:, 0:4], one_row[:, :], thr[:, 0:4],
             start=True, stop=True, wait=[DV.at()], inc=[PE.inc()])
        thr_b, nthr_b = t["thr_b"], t["nthr_b"]
        p.top("scalar", "activation", thr_b[:, :], ps_c[0][:, 0:4], AF.Copy,
              wait=[PE.at()], inc=[ACs.inc()])
        rel.use(("ps", "c", 0), "scalar", ACs)
        p.top("vector", "tensor_single_scalar", nthr_b[:, :], thr_b[:, :], -1.0, ALU.mult,
              wait=[ACs.at()], inc=[DV.inc()])
        thr_ready = DV.at()

        first = True
        for dt in range(NDT):
            for qq in range(4):
                slot = (dt * 4 + qq) % 2
                ln = L[f"LW3{slot}"]
                stl = L[f"SW3{slot}"]
                p.op("sync", "dma_start", w3f[slot], w3L[dt][:, qq * W3B:(qq + 1) * W3B],
                     wait=rel.claim(("LW3", slot), "sync"), inc=[ln.inc(16)])
                p.op("vector", "tensor_single_scalar", qa[:, 0:W3B], w3f[slot],
                     nthr_b[:, 2:3], ALU.is_lt,
                     wait=[ln.at()] + ([thr_ready] if first else []))
                first = False
                p.op("vector", "scalar_tensor_tensor", wq3[slot], w3f[slot],
                     thr_b[:, 2:3], qa[:, 0:W3B], ALU.is_gt, ALU.subtract,
                     wait=rel.claim(("wq3", slot), "vector"), inc=[DV.inc()])
                rel.use(("LW3", slot), "vector", DV)
                p.op("sync", "dma_start", w3q_s[dt][:, qq * W3B:(qq + 1) * W3B], wq3[slot],
                     wait=[DV.at()], inc=[stl.inc(16)])
                rel.use(("wq3", slot), "sync", stl)
        w3q_done = [L["SW30"].at(), L["SW31"].at()]

    row_m, row_v, row_r, row_r2, row_mr = t["row_m"], t["row_v"], t["row_r"], t["row_r2"], t["row_mr"]
    bc_r, bc_mr = t["bc_r"], t["bc_mr"]
    tmp_ln, obuf = t["tmp_ln"], t["obuf"]
    ssq, xres = t["ssq"], t["xres"]

    for half in range(2):
        for dt in range(NDT):
            w = rel.claim(("rbf", dt), "gpsimd") + rel.claim(("w3qb", dt % 2), "gpsimd")
            p.op("gpsimd", "dma_start", rbf[:, dt * 1024:(dt + 1) * 1024],
                 rT_in[dt * 128:(dt + 1) * 128, half * 1024:(half + 1) * 1024],
                 wait=w, inc=[L["LRB"].inc(16)])
        rbf_ld = L["LRB"].at()

        first_hid = True
        for blk in range(NHT):
            for m in range(2):
                src, scr = ((w1L, w1q_s), (w2L, w2q_s))[m]
                slot = (2 * blk + m) % 2
                wslot = 2 * (blk % 2) + m
                if half == 0:
                    ln = L[f"LF{slot}"]
                    p.op("scalar", "dma_start", fbuf[slot], src[blk],
                         wait=rel.claim(("LF", slot), "scalar"), inc=[ln.inc(16)])
                    p.op(QE, "tensor_single_scalar", qa[:, 0:2048],
                         fbuf[slot], nthr_b[:, m:m + 1], ALU.is_lt,
                         wait=[ln.at()])
                    p.op(QE, "scalar_tensor_tensor", wq[wslot][:, :],
                         fbuf[slot], thr_b[:, m:m + 1], qa[:, 0:2048],
                         ALU.is_gt, ALU.subtract,
                         wait=rel.claim(("wq", wslot), QE), inc=[QC.inc()])
                    rel.use(("LF", slot), QE, QC)
                    wq_ready = QC.at()
                    stl = L[f"SW12{wslot}"]
                    p.op("sync", "dma_start", scr[blk], wq[wslot][:, :],
                         wait=[QC.at()], inc=[stl.inc(16)])
                    rel.use(("wq", wslot), "sync", stl)
                else:
                    ln = L[f"LQ{wslot}"]
                    w = rel.claim(("wq", wslot), "sync")
                    if blk == 0 and m == 0:
                        w += w3q_done
                        w += [L[f"SW12{i}"].at() for i in range(4)]
                    p.op("sync", "dma_start", wq[wslot][:, :], scr[blk], wait=w,
                         inc=[ln.inc(16)])
                    wq_ready = ln.at()
                psl = (ps_a, ps_b)[m]
                for qc in range(2):
                    for dt in range(NDT):
                        w = []
                        if dt == 0:
                            w = [wq_ready, rbf_ld] + rel.claim(("ps", "ab", m, qc), "tensor")
                        p.op("tensor", "matmul", psl[qc][:, :],
                             wq[wslot][:, dt * 128:(dt + 1) * 128],
                             rbf[:, dt * 1024 + qc * 512: dt * 1024 + qc * 512 + 512],
                             start=(dt == 0), stop=(dt == NDT - 1),
                             wait=w, inc=[PE.inc()] if dt == NDT - 1 else [])
                rel.use(("wq", wslot), "tensor", PE)
            for qc in range(2):
                pe_done = PE.at()
                p.op("scalar", "activation", sg[qc][:, :], ps_a[qc][:, :],
                     AF.Sigmoid, wait=[pe_done] + rel.claim(("sg", qc), "scalar"),
                     inc=[ACs.inc()])
                rel.use(("ps", "ab", 0, qc), "scalar", ACs)
                hcol = blk * 1024 + qc * 512
                w = [ACs.at(), pe_done]
                if first_hid:
                    w += w3q_done
                    first_hid = False
                p.op("vector", "tensor_tensor", hid[:, hcol:hcol + 512],
                     sg[qc][:, :], ps_b[qc][:, :], ALU.mult,
                     wait=w, inc=[DV.inc()])
                rel.use(("ps", "ab", 1, qc), "vector", DV)
                rel.use(("sg", qc), "vector", DV)
        hid_dv = DV.at()

        for qc in range(2):
            gqc = half * 2 + qc
            for dt in range(NDT):
                lnb = L[f"LB{dt % 2}"]
                w = list(w3q_done) + rel.claim(("w3qb", dt % 2), "gpsimd")
                if gqc == half * 2 and dt < 2:
                    w.append(hid_dv)
                p.op("gpsimd", "dma_start", w3qb[dt % 2], w3q_s[dt], wait=w,
                     inc=[lnb.inc(16)])
                lnx = L[f"LX{dt % 2}"]
                p.op("gpsimd", "dma_start", xres[dt % 2][:, :],
                     rT_in[dt * 128:(dt + 1) * 128, gqc * 512:(gqc + 1) * 512],
                     wait=rel.claim(("xres", dt % 2), "gpsimd"), inc=[lnx.inc(16)])
                for ht in range(NHT):
                    w = []
                    if ht == 0:
                        w = [lnb.at(), hid_dv] + rel.claim(("ps", "c", dt % 2), "tensor")
                    p.op("tensor", "matmul", ps_c[dt % 2][:, :],
                         w3qb[dt % 2][:, ht * 128:(ht + 1) * 128],
                         hid[:, ht * 1024 + qc * 512: ht * 1024 + qc * 512 + 512],
                         start=(ht == 0), stop=(ht == NHT - 1),
                         wait=w, inc=[PE.inc()] if ht == NHT - 1 else [])
                rel.use(("w3qb", dt % 2), "tensor", PE)
                p.op("vector", "tensor_tensor", S[:, dt * 512:(dt + 1) * 512],
                     ps_c[dt % 2][:, :], xres[dt % 2][:, :], ALU.add,
                     wait=[PE.at(), lnx.at()] + rel.claim(("S", dt), "vector"),
                     inc=[DV.inc()])
                rel.use(("ps", "c", dt % 2), "vector", DV)
                rel.use(("xres", dt % 2), "vector", DV)
            s_done = DV.at()
            stats_w = [s_done] + rel.claim(("ps", "s1"), "tensor") + rel.claim(("ps", "s2"), "tensor")
            for dt in range(NDT):
                p.op("scalar", "activation", ssq[dt % 2][:, :],
                     S[:, dt * 512:(dt + 1) * 512], AF.Square,
                     wait=([s_done] if dt == 0 else []) + rel.claim(("ssq", dt % 2), "scalar"),
                     inc=[ACs.inc()])
                rel.use(("S", dt), "scalar", ACs)
                p.op("tensor", "matmul", ps_s1[:, :], one_col[:, :],
                     S[:, dt * 512:(dt + 1) * 512], start=(dt == 0), stop=(dt == NDT - 1),
                     wait=(stats_w if dt == 0 else []), inc=[PE.inc()])
                p.op("tensor", "matmul", ps_s2[:, :], one_col[:, :], ssq[dt % 2][:, :],
                     start=(dt == 0), stop=(dt == NDT - 1),
                     wait=[ACs.at()], inc=[PE.inc()])
                rel.use(("ssq", dt % 2), "tensor", PE)
                rel.use(("S", dt), "tensor", PE)
            p.wait("vector", *PE.at())
            p.top("vector", "tensor_single_scalar", row_m[:, :], ps_s1[:, :], 1.0 / DM, ALU.mult)
            p.top("vector", "tensor_single_scalar", row_v[:, :], ps_s2[:, :], 1.0 / DM, ALU.mult)
            p.top("vector", "tensor_tensor", row_r[:, :], row_m[:, :], row_m[:, :], ALU.mult)
            p.top("vector", "tensor_tensor", row_v[:, :], row_v[:, :], row_r[:, :], ALU.subtract,
                  inc=[DV.inc()])
            rel.use(("ps", "s1"), "vector", DV)
            rel.use(("ps", "s2"), "vector", DV)
            p.top("scalar", "activation", row_r[:, :], row_v[:, :], AF.Ln,
                  bias=t["crow"][0:1, 0:1], wait=[DV.at()])
            p.top("scalar", "activation", row_r2[:, :], row_r[:, :], AF.Exp, scale=-0.5,
                  inc=[ACs.inc()])
            p.top("vector", "tensor_tensor", row_mr[:, :], row_m[:, :], row_r2[:, :], ALU.mult,
                  wait=[ACs.at()], inc=[DV.inc()])
            p.op("tensor", "matmul", ps_c[0][:, :], one_row[:, :], row_r2[:, :],
                 start=True, stop=True,
                 wait=[DV.at()] + rel.claim(("ps", "c", 0), "tensor"), inc=[PE.inc()])
            p.op("tensor", "matmul", ps_c[1][:, :], one_row[:, :], row_mr[:, :],
                 start=True, stop=True,
                 wait=rel.claim(("ps", "c", 1), "tensor"), inc=[PE.inc()])
            p.op("scalar", "activation", bc_r[:, :], ps_c[0][:, :], AF.Copy,
                 wait=[PE.at()], inc=[ACs.inc()])
            rel.use(("ps", "c", 0), "scalar", ACs)
            p.op("scalar", "activation", bc_mr[:, :], ps_c[1][:, :], AF.Copy,
                 inc=[ACs.inc()])
            rel.use(("ps", "c", 1), "scalar", ACs)
            bc_done = ACs.at()
            p.wait("vector", *bc_done)
            for dt in range(NDT):
                p.op("vector", "tensor_tensor", tmp_ln[dt % 2][:, :],
                     S[:, dt * 512:(dt + 1) * 512], bc_r[:, :], ALU.mult,
                     wait=rel.claim(("tmp_ln", dt % 2), "vector"))
                p.op("vector", "tensor_tensor", obuf[dt % 2][:, :], tmp_ln[dt % 2][:, :],
                     bc_mr[:, :], ALU.subtract,
                     wait=rel.claim(("obuf", dt % 2), "vector"), inc=[DV.inc()])
                rel.use(("S", dt), "vector", DV)
                sto = L[f"SOB{dt % 2}"]
                p.op("sync", "dma_start",
                     out_hbm[dt * 128:(dt + 1) * 128, gqc * 512:(gqc + 1) * 512],
                     obuf[dt % 2][:, :], wait=[DV.at()], inc=[sto.inc(16)])
                rel.use(("obuf", dt % 2), "sync", sto)
                rel.use(("tmp_ln", dt % 2), "sync", sto)
        for dt in range(NDT):
            rel.use(("rbf", dt), "tensor", PE)


FFN_LANES = (["MISC", "LRB"] + [f"LF{i}" for i in range(2)] + [f"LW3{i}" for i in range(2)]
             + [f"LQ{i}" for i in range(4)] + [f"LB{i}" for i in range(2)]
             + [f"LX{i}" for i in range(2)] + [f"SW3{i}" for i in range(2)]
             + [f"SW12{i}" for i in range(4)] + [f"SOB{i}" for i in range(2)])


def alloc_ffn_tensors(nc, es, out_f32):
    t = {}
    sb = lambda name, shape, dt: es.enter_context(nc.sbuf_tensor(name, shape, dt))
    t["one_col"] = sb("one_col", [128, 1], F32)
    t["one_row"] = sb("one_row", [1, 128], F32)
    t["crow"] = sb("crow", [1, 8], F32)
    t["gpart"] = sb("gpart", [128, 160], F32)
    t["gsum"] = sb("gsum", [1, 4], F32)
    t["thr"] = sb("thr", [1, 4], F32)
    t["thr_b"] = sb("thr_b", [128, 4], F32)
    t["nthr_b"] = sb("nthr_b", [128, 4], F32)
    t["rbf"] = sb("rbf", [128, NDT * 1024], BF16)
    t["hid"] = sb("hid", [128, NHT * 1024], BF16)
    t["S"] = sb("S", [128, NDT * 512], F32)
    t["qa"] = sb("qa", [128, 2048], BF16)
    t["wq"] = [sb(f"wq{i}", [128, 2048], BF16) for i in range(4)]
    t["sg"] = [sb(f"sg{i}", [128, 512], BF16) for i in range(2)]
    t["ssq"] = [sb(f"ssq{i}", [128, 512], F32) for i in range(2)]
    t["xres"] = [sb(f"xres{i}", [128, 512], F32) for i in range(2)]
    t["row_a"] = sb("row_a", [1, 512], F32)
    t["row_m"] = sb("row_m", [1, 512], F32)
    t["row_v"] = sb("row_v", [1, 512], F32)
    t["row_r"] = sb("row_r", [1, 512], F32)
    t["row_r2"] = sb("row_r2", [1, 512], F32)
    t["row_mr"] = sb("row_mr", [1, 512], F32)
    t["bc_r"] = sb("bc_r", [128, 512], F32)
    t["bc_mr"] = sb("bc_mr", [128, 512], F32)
    t["tmp_ln"] = [sb(f"tmp_ln{i}", [128, 512], F32) for i in range(2)]
    odt = F32 if out_f32 else BF16
    t["obuf"] = [sb(f"obuf{i}", [128, 512], odt) for i in range(2)]
    ps = lambda name, shape: es.enter_context(nc.psum_tensor(name, shape, F32))
    t["ps_a0"] = ps("ps_a0", [128, 512])
    t["ps_a1"] = ps("ps_a1", [128, 512])
    t["ps_b0"] = ps("ps_b0", [128, 512])
    t["ps_b1"] = ps("ps_b1", [128, 512])
    t["ps_c0"] = ps("ps_c0", [128, 512])
    t["ps_c1"] = ps("ps_c1", [128, 512])
    t["ps_s1"] = ps("ps_s1", [1, 512])
    t["ps_s2"] = ps("ps_s2", [1, 512])
    t["w1q_s"] = nc.dram_tensor("w1q_s", [NHT, 128, NDT * 128], BF16, kind="Internal").ap()
    t["w2q_s"] = nc.dram_tensor("w2q_s", [NHT, 128, NDT * 128], BF16, kind="Internal").ap()
    t["w3q_s"] = nc.dram_tensor("w3q_s", [NDT, 128, NHT * 128], BF16, kind="Internal").ap()
    return t


def build_ffn_stage(out_f32, debug=False):
    nc = bass.Bass("TRN2", target_bir_lowering=False, debug=False)
    rT_in = nc.dram_tensor("rT_in", [DM, NTOK], F32, kind="ExternalInput").ap()
    w1L = nc.dram_tensor("w1L", [NHT, 128, NDT * 128], F32, kind="ExternalInput").ap()
    w2L = nc.dram_tensor("w2L", [NHT, 128, NDT * 128], F32, kind="ExternalInput").ap()
    w3L = nc.dram_tensor("w3L", [NDT, 128, NHT * 128], F32, kind="ExternalInput").ap()
    ones_col_in = nc.dram_tensor("ones_col_in", [128, 1], F32, kind="ExternalInput").ap()
    ones_row_in = nc.dram_tensor("ones_row_in", [1, 128], F32, kind="ExternalInput").ap()
    consts_in = nc.dram_tensor("consts_in", [1, 8], F32, kind="ExternalInput").ap()
    odt = F32 if out_f32 else BF16
    out_hbm = nc.dram_tensor("rT_out", [DM, NTOK], odt, kind="ExternalOutput").ap()
    with ExitStack() as es:
        t = alloc_ffn_tensors(nc, es, out_f32)
        c = {k: Ctr(es, nc, k) for k in ("PE", "AC", "DV")}
        c["lanes"] = {k: Ctr(es, nc, k) for k in FFN_LANES}
        p = Prog(nc)
        p.chains = {e: Ctr(es, nc, "CH_" + e) for e in ("vector", "scalar", "tensor", "gpsimd")}
        M = c["lanes"]["MISC"]
        p.op("sync", "dma_start", t["one_col"][:, :], ones_col_in, inc=[M.inc(16)])
        p.op("sync", "dma_start", t["one_row"][:, :], ones_row_in, inc=[M.inc(16)])
        p.op("sync", "dma_start", t["crow"][:, :], consts_in, inc=[M.inc(16)])
        emit_ffn(nc, p, c, t, rT_in, w1L, w2L, w3L, out_hbm, out_f32)
        if debug:
            dbg_thr = nc.dram_tensor("dbg_thr", [128, 4], F32, kind="ExternalOutput").ap()
            dbg_hid = nc.dram_tensor("dbg_hid", [128, NHT * 1024], BF16, kind="ExternalOutput").ap()
            dbg_S = nc.dram_tensor("dbg_S", [128, NDT * 512], F32, kind="ExternalOutput").ap()
            dbg_gp = nc.dram_tensor("dbg_gp", [128, 160], F32, kind="ExternalOutput").ap()
            SOB = c["lanes"]["SOB0"]
            w = [c["lanes"][f"SOB{i}"].at() for i in range(2)]
            p.op("sync", "dma_start", dbg_thr, t["thr_b"][:, :], wait=w, inc=[SOB.inc(16)])
            p.op("sync", "dma_start", dbg_hid, t["hid"][:, :], inc=[SOB.inc(16)])
            p.op("sync", "dma_start", dbg_S, t["S"][:, :], inc=[SOB.inc(16)])
            p.op("sync", "dma_start", dbg_gp, t["gpart"][:, :], inc=[SOB.inc(16)])
            dbg_row = nc.dram_tensor("dbg_row", [1, 512], F32, kind="ExternalOutput").ap()
            p.op("sync", "dma_start", dbg_row, t["row_a"][:, :], inc=[SOB.inc(16)])
            dbg_m = nc.dram_tensor("dbg_m", [1, 512], F32, kind="ExternalOutput").ap()
            dbg_r2 = nc.dram_tensor("dbg_r2", [1, 512], F32, kind="ExternalOutput").ap()
            dbg_bcr = nc.dram_tensor("dbg_bcr", [128, 512], F32, kind="ExternalOutput").ap()
            p.op("sync", "dma_start", dbg_m, t["row_m"][:, :], inc=[SOB.inc(16)])
            p.op("sync", "dma_start", dbg_r2, t["row_r2"][:, :], inc=[SOB.inc(16)])
            p.op("sync", "dma_start", dbg_bcr, t["bc_r"][:, :], inc=[SOB.inc(16)])
        for k in FFN_LANES:
            if k.startswith("S"):
                p.wait("sync", *c["lanes"][k].at())
        p.finish()
    return nc



NH = 16
HD = 128
KVC = 512
QCC = 1536
SCALE_QK = 0.08838834764831845


def emit_ln_chunk(p, c, t, rel, S, gqc, out_hbm, lane_pref):
    PE, ACs, DV = c["PE"], c["AC"], c["DV"]
    L = c["lanes"]
    one_col, one_row = t["one_col"], t["one_row"]
    ssq = t["ssq"]
    row_m, row_v, row_r, row_r2, row_mr = t["row_m"], t["row_v"], t["row_r"], t["row_r2"], t["row_mr"]
    bc_r, bc_mr = t["bc_r"], t["bc_mr"]
    tmp_ln, obuf = t["tmp_ln"], t["obuf"]
    ps_s1, ps_s2 = t["ps_s1"], t["ps_s2"]
    ps_c = [t["ps_c0"], t["ps_c1"]]
    s_done = c["DV"].at()
    stats_w = [s_done] + rel.claim(("ps", "s1"), "tensor") + rel.claim(("ps", "s2"), "tensor")
    for dt in range(NDT):
        p.op("scalar", "activation", ssq[dt % 2], S[:, dt * 512:(dt + 1) * 512], AF.Square,
             wait=([s_done] if dt == 0 else []) + rel.claim(("ssq", dt % 2), "scalar"),
             inc=[ACs.inc()])
        rel.use(("S", dt), "scalar", ACs)
        p.op("tensor", "matmul", ps_s1[:, :], one_col[:, :],
             S[:, dt * 512:(dt + 1) * 512], start=(dt == 0), stop=(dt == NDT - 1),
             wait=(stats_w if dt == 0 else []), inc=[PE.inc()])
        p.op("tensor", "matmul", ps_s2, one_col[:, :], ssq[dt % 2],
             start=(dt == 0), stop=(dt == NDT - 1),
             wait=[ACs.at()], inc=[PE.inc()])
        rel.use(("ssq", dt % 2), "tensor", PE)
        rel.use(("S", dt), "tensor", PE)
    p.wait("vector", *PE.at())
    p.top("vector", "tensor_single_scalar", row_m, ps_s1[:, :], 1.0 / DM, ALU.mult)
    p.top("vector", "tensor_single_scalar", row_v, ps_s2, 1.0 / DM, ALU.mult)
    p.top("vector", "tensor_tensor", row_r, row_m, row_m, ALU.mult)
    p.top("vector", "tensor_tensor", row_v, row_v, row_r, ALU.subtract, inc=[DV.inc()])
    rel.use(("ps", "s1"), "vector", DV)
    rel.use(("ps", "s2"), "vector", DV)
    p.top("scalar", "activation", row_r, row_v, AF.Ln, bias=t["crow"][0:1, 0:1],
          wait=[DV.at()])
    p.top("scalar", "activation", row_r2, row_r, AF.Exp, scale=-0.5, inc=[ACs.inc()])
    p.top("vector", "tensor_tensor", row_mr, row_m, row_r2, ALU.mult,
          wait=[ACs.at()], inc=[DV.inc()])
    p.op("tensor", "matmul", ps_c[0][:, :], one_row[:, :], row_r2,
         start=True, stop=True,
         wait=[DV.at()] + rel.claim(("ps", "c", 0), "tensor"), inc=[PE.inc()])
    p.op("tensor", "matmul", ps_c[1][:, :], one_row[:, :], row_mr,
         start=True, stop=True,
         wait=rel.claim(("ps", "c", 1), "tensor"), inc=[PE.inc()])
    p.op("scalar", "activation", bc_r, ps_c[0][:, :], AF.Copy,
         wait=[PE.at()], inc=[ACs.inc()])
    rel.use(("ps", "c", 0), "scalar", ACs)
    p.op("scalar", "activation", bc_mr, ps_c[1][:, :], AF.Copy, inc=[ACs.inc()])
    rel.use(("ps", "c", 1), "scalar", ACs)
    p.wait("vector", *ACs.at())
    for dt in range(NDT):
        p.op("vector", "tensor_tensor", tmp_ln[dt % 2],
             S[:, dt * 512:(dt + 1) * 512], bc_r, ALU.mult,
             wait=rel.claim(("tmp_ln", dt % 2), "vector"))
        p.op("vector", "tensor_tensor", obuf[dt % 2], tmp_ln[dt % 2],
             bc_mr, ALU.subtract,
             wait=rel.claim(("obuf", dt % 2), "vector"), inc=[DV.inc()])
        rel.use(("S", dt), "vector", DV)
        sto = L[f"{lane_pref}{dt % 2}"]
        p.op("sync", "dma_start",
             out_hbm[dt * 128:(dt + 1) * 128, gqc * 512:(gqc + 1) * 512],
             obuf[dt % 2], wait=[DV.at()], inc=[sto.inc(16)])
        rel.use(("obuf", dt % 2), "sync", sto)
        rel.use(("tmp_ln", dt % 2), "sync", sto)


def emit_mla(nc, p, c, t, io, head_hook=None, qn_hook=None):
    PE, ACs, DV = c["PE"], c["AC"], c["DV"]
    L = c["lanes"]
    rel = Rel()
    one_col, one_row = t["one_col"], t["one_row"]
    xT = io["xT"]
    wdkv = t["wdkv"]
    wdqb = t["wdqb"]
    wukvb = t["wukvb"]
    wqr, wkr = t["wqr"], t["wkr"]
    bdkv, bdq, buk, buq = t["bdkv"], t["bdq"], t["buk"], t["buq"]
    buv_h, one_row16, one_col16 = t["buv_h"], t["one_row16"], t["one_col16"]
    ckv = t["ckv"]
    cqe, cqb, cq_s = t["cqe"], t["cqb"], t["cq_s"]
    Vh, Qh = t["Vh"], t["Qh"]
    khraw = t["khraw"]
    Kc, kcol, ckvbar, ckvsum = t["Kc"], t["kcol"], t["ckvbar"], t["ckvsum"]
    arena = t["arena"]
    kcos, ksin = arena[0:32, :], arena[32:64, :]
    qcos, qsin = t["qcos"], t["qsin"]
    wuqb = t["wuqb"]
    probs = t["probs"]
    lp = t["lp"]
    rope_t = t["rope_t"]
    mq, mk, prod = t["mq"], t["mk"], t["prod"]
    negc, negc_b = t["negc"], t["negc_b"]
    bcl, linv = t["bcl"], t["linv"]
    ctxo = t["ctxo"]
    ps_sc = [t["ps_sc0"], t["ps_sc1"]]
    ps_ctx = [t["ps_ctx0"], t["ps_ctx1"]]
    ps_p = [t["ps_p0"], t["ps_p1"]]
    ps_l, ps_bc = t["ps_l"], t["ps_bc"]
    ctx_s = t["ctx_s"]
    xbf = t["xbf"]
    misc_ready = L["MISC"].at()
    LPE, LPC = "vector", DV
    NSC = SEQ // 512
    NKT = SEQ // 128

    for sc in range(NSC):
        ln = L[f"LX{sc % 2}"]
        w = rel.claim(("xbf", sc % 2), "gpsimd")
        if sc < 2:
            w.append(misc_ready)
        for dt in range(NDT):
            p.op("gpsimd", "dma_start", xbf[sc % 2][:, dt * 512:(dt + 1) * 512],
                 xT[dt * 128:(dt + 1) * 128, sc * 512:(sc + 1) * 512],
                 wait=(w if dt == 0 else []), inc=[ln.inc(16)])
        ld = ln.at()
        for ct in range(4):
            bank = (sc * 4 + ct) % 2
            for dt in range(NDT):
                w2 = []
                if dt == 0:
                    w2 = [ld] + rel.claim(("ps", "p", bank), "tensor")
                p.op("tensor", "matmul", ps_p[bank][:, :],
                     wdkv[:, dt * 512 + ct * 128: dt * 512 + (ct + 1) * 128],
                     xbf[sc % 2][:, dt * 512:(dt + 1) * 512],
                     start=(dt == 0), stop=(dt == NDT - 1),
                     wait=w2, inc=[PE.inc()] if dt == NDT - 1 else [])
            p.op("vector", "tensor_single_scalar",
                 ckv[:, ct * SEQ + sc * 512: ct * SEQ + (sc + 1) * 512],
                 ps_p[bank][:, :], bdkv[:, ct:ct + 1], ALU.add,
                 wait=[PE.at()], inc=[DV.inc()])
            rel.use(("ps", "p", bank), "vector", DV)
        rel.use(("xbf", sc % 2), "tensor", PE)

    for qn in range(4):
        ln = L[f"LX{qn % 2}"]
        w = rel.claim(("xbf", qn % 2), "gpsimd")
        for dt in range(NDT):
            p.op("gpsimd", "dma_start", xbf[qn % 2][:, dt * 512:(dt + 1) * 512],
                 xT[dt * 128:(dt + 1) * 128, qn * 512:(qn + 1) * 512],
                 wait=(w if dt == 0 else []), inc=[ln.inc(16)])
        ld = ln.at()
        for ct in range(12):
            lw = L[f"LDQ{ct % 2}"]
            p.op("gpsimd", "dma_start", wdqb[ct % 2], io["wdqT4"][ct],
                 wait=rel.claim(("wdqb", ct % 2), "gpsimd"), inc=[lw.inc(16)])
            bank = ct % 2
            for dt in range(NDT):
                w2 = []
                if dt == 0:
                    w2 = [ld, lw.at()] + rel.claim(("ps", "sc", bank), "tensor")
                p.op("tensor", "matmul", ps_sc[bank][:, :],
                     wdqb[ct % 2][:, dt * 128:(dt + 1) * 128],
                     xbf[qn % 2][:, dt * 512:(dt + 1) * 512],
                     start=(dt == 0), stop=(dt == NDT - 1),
                     wait=w2, inc=[PE.inc()] if dt == NDT - 1 else [])
            rel.use(("wdqb", ct % 2), "tensor", PE)
            p.op("vector", "tensor_single_scalar", cqe[ct % 2], ps_sc[bank][:, :],
                 bdq[:, ct:ct + 1], ALU.add,
                 wait=[PE.at()] + rel.claim(("cqe", ct % 2), "vector"),
                 inc=[DV.inc()])
            rel.use(("ps", "sc", bank), "vector", DV)
            scq = L[f"SCQ{ct % 2}"]
            p.op("sync", "dma_start", cq_s[ct][:, qn * 512:(qn + 1) * 512],
                 cqe[ct % 2], wait=[DV.at()], inc=[scq.inc(16)])
            rel.use(("cqe", ct % 2), "sync", scq)
        rel.use(("xbf", qn % 2), "tensor", PE)
    ckv_done = DV.at()
    p.wait("vector", *ckv_done)
    for ct in range(4):
        p.top("vector", "tensor_reduce", ckvsum[:, :], ckv[:, ct * SEQ:(ct + 1) * SEQ],
              AX.X, ALU.add)
        p.top("vector", "tensor_single_scalar", ckvbar[:, ct:ct + 1], ckvsum[:, :],
              1.0 / SEQ, ALU.mult, inc=[DV.inc()] if ct == 3 else [])
    ckvbar_dv = DV.at()
    cq_done = [L["SCQ0"].at(), L["SCQ1"].at()]
    p1_pe = PE.at()

    for h in range(NH):
        lnq = L[f"LUQ{h % 2}"]
        wqw = rel.claim(("wuqb", h % 2), "gpsimd")
        if h < 2:
            wqw.append(p1_pe)
            wqw += list(cq_done)
        p.op("gpsimd", "dma_start", wuqb[h % 2], io["wuqT4"][h], wait=wqw,
             inc=[lnq.inc(16)])
        lnk = L[f"LUK{h % 2}"]
        wkw = rel.claim(("wukvb", h % 2), "gpsimd")
        if h < 2:
            wkw.append(p1_pe)
        p.op("gpsimd", "dma_start", wukvb[h % 2], io["wukvT4"][h], wait=wkw,
             inc=[lnk.inc(16)])
        lbv = L[f"LBV{h % 2}"]
        p.op("gpsimd", "dma_start", buv_h[h % 2][:, :],
             io["buv_in"][0:1, h * 128:(h + 1) * 128],
             wait=rel.claim(("buv_h", h % 2), "gpsimd"), inc=[lbv.inc(16)])

        for qn in range(4):
            bank = qn % 2
            lcq = L[f"LCQ{qn % 2}"]
            wq0 = rel.claim(("cqb", qn % 2), "sync")
            if h == 0 and qn < 2:
                wq0 += list(cq_done) + [p1_pe]
            for ct in range(12):
                p.op("sync", "dma_start", cqb[qn % 2][:, ct * 512:(ct + 1) * 512],
                     cq_s[ct][:, qn * 512:(qn + 1) * 512],
                     wait=(wq0 if ct == 0 else []), inc=[lcq.inc(16)])
            for ct in range(12):
                w2 = []
                if ct == 0:
                    w2 = [lnq.at(), lcq.at()] + rel.claim(("ps", "p", bank), "tensor")
                p.op("tensor", "matmul", ps_p[bank][:, :],
                     wuqb[h % 2][:, ct * 128:(ct + 1) * 128],
                     cqb[qn % 2][:, ct * 512:(ct + 1) * 512],
                     start=(ct == 0), stop=(ct == 11),
                     wait=w2, inc=[PE.inc()] if ct == 11 else [])
            rel.use(("cqb", qn % 2), "tensor", PE)
            p.op("vector", "tensor_single_scalar", Qh[:, qn * 512:(qn + 1) * 512],
                 ps_p[bank][:, :], buq[:, h:h + 1], ALU.add,
                 wait=[PE.at()], inc=[DV.inc()])
            rel.use(("ps", "p", bank), "vector", DV)
            p.op("tensor", "matmul", ps_bc[0:64, :], wqr[:, :],
                 Qh[:, qn * 512:(qn + 1) * 512], start=True, stop=True,
                 wait=[DV.at()] + rel.claim(("ps", "bc"), "tensor"),
                 inc=[PE.inc()])
            p.op("vector", "tensor_tensor", rope_t[0], ps_bc[0:32, :],
                 qcos[:, qn * 512:(qn + 1) * 512], ALU.mult,
                 wait=[PE.at(), misc_ready] + rel.claim(("rope_t", 0), "vector"))
            p.op("vector", "tensor_tensor", rope_t[1], ps_bc[32:64, :],
                 qsin[:, qn * 512:(qn + 1) * 512], ALU.mult,
                 wait=rel.claim(("rope_t", 1), "vector"), inc=[DV.inc()])
            rel.use(("ps", "bc"), "vector", DV)
            p.op("vector", "tensor_tensor", Qh[0:32, qn * 512:(qn + 1) * 512],
                 rope_t[0], rope_t[1], ALU.subtract, inc=[DV.inc()])
            rel.use(("rope_t", 0), "vector", DV)
            rel.use(("rope_t", 1), "vector", DV)
        p.op("vector", "memset", Qh[32:64, :], 0.0, inc=[DV.inc()])
        qh_dv = DV.at()

        for ct in range(4):
            w2 = []
            if ct == 0:
                w2 = [lnk.at(), ckvbar_dv] + rel.claim(("ps", "bc"), "tensor")
            p.op("tensor", "matmul", ps_bc[0:128, 0:1],
                 wukvb[h % 2][:, ct * 128:(ct + 1) * 128], ckvbar[:, ct:ct + 1],
                 start=(ct == 0), stop=(ct == 3),
                 wait=w2, inc=[PE.inc()] if ct == 3 else [])
        p.top("vector", "tensor_tensor", kcol[:, :], ps_bc[0:128, 0:1],
              buk[:, h:h + 1], ALU.add,
              wait=[PE.at()] + rel.claim(("kcol",), "vector"), inc=[DV.inc()])
        rel.use(("ps", "bc"), "vector", DV)
        p.op("vector", "memset", Kc[32:64, :], 0.0,
             wait=rel.claim(("Kc",), "vector"))

        for sc in range(NSC):
            bank = sc % 2
            for ct in range(4):
                w2 = []
                if ct == 0:
                    w2 = [lnk.at(), ckv_done] + rel.claim(("ps", "p", bank), "tensor")
                p.op("tensor", "matmul", ps_p[bank][:, :],
                     wukvb[h % 2][:, ct * 128:(ct + 1) * 128],
                     ckv[:, ct * SEQ + sc * 512: ct * SEQ + (sc + 1) * 512],
                     start=(ct == 0), stop=(ct == 3),
                     wait=w2, inc=[PE.inc()] if ct == 3 else [])
            p.op("vector", "tensor_single_scalar", khraw[sc % 2][:, :],
                 ps_p[bank][:, :], buk[:, h:h + 1], ALU.add,
                 wait=[PE.at()] + rel.claim(("khraw", sc % 2), "vector"),
                 inc=[DV.inc()])
            rel.use(("ps", "p", bank), "vector", DV)
            p.op("tensor", "matmul", ps_bc[0:64, :], wkr[:, :],
                 khraw[sc % 2][:, :], start=True, stop=True,
                 wait=[DV.at()] + rel.claim(("ps", "bc"), "tensor"),
                 inc=[PE.inc()])
            rel.use(("khraw", sc % 2), "tensor", PE)
            p.op("vector", "tensor_tensor", rope_t[0], ps_bc[0:32, :],
                 kcos[:, sc * 512:(sc + 1) * 512], ALU.mult,
                 wait=[PE.at()] + rel.claim(("rope_t", 0), "vector"))
            p.op("vector", "tensor_tensor", rope_t[1], ps_bc[32:64, :],
                 ksin[:, sc * 512:(sc + 1) * 512], ALU.mult,
                 wait=rel.claim(("rope_t", 1), "vector"), inc=[DV.inc()])
            rel.use(("ps", "bc"), "vector", DV)
            p.op("vector", "tensor_tensor", Kc[0:32, sc * 512:(sc + 1) * 512],
                 rope_t[0], rope_t[1], ALU.subtract, inc=[DV.inc()])
            rel.use(("rope_t", 0), "vector", DV)
            rel.use(("rope_t", 1), "vector", DV)
            p.op("vector", "tensor_single_scalar", Kc[64:128, sc * 512:(sc + 1) * 512],
                 khraw[sc % 2][64:128, :], kcol[64:128, 0:1], ALU.subtract,
                 inc=[DV.inc()])
            rel.use(("khraw", sc % 2), "vector", DV)
        kh_dv = DV.at()

        p.wait("vector", *ACs.at())
        p.wait("vector", *qh_dv)
        p.wait("vector", *kh_dv)
        p.top("vector", "tensor_reduce", mq[:, :], Qh[:, :], AX.X, ALU.max,
              apply_absolute_value=True)
        p.top("vector", "tensor_reduce", mk[:, :], Kc[:, :], AX.X, ALU.max,
              apply_absolute_value=True)
        p.top("vector", "tensor_tensor", prod[:, :], mq[:, :], mk[:, :], ALU.mult, inc=[DV.inc()])
        bound_dv = DV.at()

        for g in range(16):
            bank = g % 2
            first_w = [lnk.at(), ckv_done, misc_ready] + rel.claim(("ps", "sc", bank), "tensor")
            for sub in range(4):
                st = g * 4 + sub
                for ct in range(4):
                    w2 = first_w if (sub == 0 and ct == 0) else []
                    p.op("tensor", "matmul", ps_sc[bank][:, sub * 128:(sub + 1) * 128],
                         ckv[:, ct * SEQ + st * 128: ct * SEQ + (st + 1) * 128],
                         wukvb[h % 2][:, 512 + ct * 128: 512 + (ct + 1) * 128],
                         start=(ct == 0), stop=False, wait=w2)
                p.op("tensor", "matmul", ps_sc[bank][:, sub * 128:(sub + 1) * 128],
                     one_row16[:, :], buv_h[h % 2][:, :],
                     start=False, stop=True,
                     wait=([lbv.at()] if sub == 0 else []),
                     inc=[PE.inc()] if sub == 3 else [])
            p.op("scalar", "activation", Vh[:, g * 512:(g + 1) * 512],
                 ps_sc[bank][:, :], AF.Copy, wait=[PE.at()], inc=[ACs.inc()])
            rel.use(("ps", "sc", bank), "scalar", ACs)
        rel.use(("wuqb", h % 2), "tensor", PE)
        rel.use(("wukvb", h % 2), "tensor", PE)
        rel.use(("buv_h", h % 2), "tensor", PE)

        p.op("tensor", "matmul", ps_l[0:1, 0:1], one_col[:, :], prod[:, :],
             start=True, stop=True, wait=[bound_dv] + rel.claim(("ps", "l"), "tensor"),
             inc=[PE.inc()])
        p.top("vector", "tensor_scalar", negc[:, :], ps_l[0:1, 0:1],
              -SCALE_QK, 20.0, ALU.mult, ALU.add, wait=[PE.at()])
        p.top("vector", "tensor_single_scalar", t["negc2"][:, :],
              negc[:, :], -55.0, ALU.max, inc=[DV.inc()])
        rel.use(("ps", "l"), "vector", DV)
        p.op("tensor", "matmul", ps_bc[0:128, 0:1], one_row[:, :], t["negc2"][:, :],
             start=True, stop=True, wait=[DV.at()] + rel.claim(("ps", "bc"), "tensor"),
             inc=[PE.inc()])
        p.top("scalar", "activation", negc_b[:, :], ps_bc[0:128, 0:1], AF.Copy,
              wait=[PE.at()], inc=[ACs.inc()])
        rel.use(("ps", "bc"), "scalar", ACs)
        negc_ac = ACs.at()

        NP = NKT // 2
        ps_pair = t["ps_pair"]
        pend_a = pend_b = None
        for qn in range(4):
            cbank = qn % 2
            pv_w0 = rel.claim(("ps", "ctx", cbank), "tensor")
            prev_exp = None
            for j in range(NP):
                if j == 1 and pend_a is not None:
                    pend_a()
                    pend_a = None
                if j == 9 and pend_b is not None:
                    pend_b()
                    pend_b = None
                pb = j % 2
                w2 = (rel.claim(("ps", "scp", pb), "tensor")
                      + rel.claim(("ps", "sc", pb), "tensor")
                      + rel.claim(("ps", "p", pb), "tensor"))
                if j == 0 and qn == 0:
                    w2 += [kh_dv, qh_dv]
                p.op("tensor", "matmul", ps_pair[pb][:, 0:512],
                     Kc[:, (2 * j) * 128:(2 * j + 1) * 128],
                     Qh[:, qn * 512:(qn + 1) * 512],
                     start=True, stop=True, wait=w2, inc=[PE.inc()])
                p.op("tensor", "matmul", ps_pair[pb][:, 512:1024],
                     Kc[:, (2 * j + 1) * 128:(2 * j + 2) * 128],
                     Qh[:, qn * 512:(qn + 1) * 512],
                     start=True, stop=True, inc=[PE.inc()])
                qk_pe = PE.at()
                if j > 0:
                    i = j - 1
                    p.op("tensor", "matmul", ps_ctx[cbank][:, :],
                         Vh[:, (2 * i) * 128:(2 * i + 1) * 128],
                         probs[i % 2][:, 0:512],
                         start=(i == 0), stop=False,
                         wait=[prev_exp] + (pv_w0 if i == 0 else []),
                         inc=[PE.inc()])
                    p.op("tensor", "matmul", ps_ctx[cbank][:, :],
                         Vh[:, (2 * i + 1) * 128:(2 * i + 2) * 128],
                         probs[i % 2][:, 512:1024],
                         start=False, stop=False, inc=[PE.inc()])
                    rel.use(("probs", i % 2), "tensor", PE)
                we = [qk_pe] + rel.claim(("probs", j % 2), "scalar")
                if j == 0 and qn == 0:
                    we.append(negc_ac)
                p.op("scalar", "activation", probs[j % 2][:, :], ps_pair[pb][:, 0:1024],
                     AF.Exp, bias=negc_b[:, 0:1], scale=SCALE_QK,
                     wait=we, inc=[ACs.inc()])
                prev_exp = ACs.at()
                rel.use(("ps", "scp", pb), "scalar", ACs)
                if j == 0:
                    p.op(LPE, "tensor_copy", lp[1][:, :], probs[j % 2][:, :],
                         wait=[ACs.at()] + rel.claim(("lp", 1), LPE), inc=[LPC.inc()])
                else:
                    p.op(LPE, "tensor_tensor", lp[(j + 1) % 2][:, :],
                         lp[j % 2][:, :], probs[j % 2][:, :], ALU.add,
                         wait=[ACs.at()] + rel.claim(("lp", (j + 1) % 2), LPE),
                         inc=[LPC.inc()])
                rel.use(("probs", j % 2), LPE, LPC)
            i = NP - 1
            p.op("tensor", "matmul", ps_ctx[cbank][:, :],
                 Vh[:, (2 * i) * 128:(2 * i + 1) * 128], probs[i % 2][:, 0:512],
                 start=False, stop=False, wait=[prev_exp], inc=[PE.inc()])
            p.op("tensor", "matmul", ps_ctx[cbank][:, :],
                 Vh[:, (2 * i + 1) * 128:(2 * i + 2) * 128], probs[i % 2][:, 512:1024],
                 start=False, stop=True, inc=[PE.inc()])
            rel.use(("probs", i % 2), "tensor", PE)

            def mk_a():
                def f():
                    p.op("tensor", "matmul", ps_l[0:1, :], one_col16[:, :],
                         lp[0][:, 0:512], start=True, stop=False,
                         wait=[LPC.at()] + rel.claim(("ps", "l"), "tensor"),
                         inc=[PE.inc()])
                    p.op("tensor", "matmul", ps_l[0:1, :], one_col16[:, :],
                         lp[0][:, 512:1024], start=False, stop=True, inc=[PE.inc()])
                    rel.use(("lp", 0), "tensor", PE)
                    p.top("vector", "reciprocal", linv, ps_l[0:1, :],
                          wait=[PE.at()], inc=[DV.inc()])
                    rel.use(("ps", "l"), "vector", DV)
                return f

            def mk_b(qn=qn, cbank=cbank):
                def f():
                    p.op("tensor", "matmul", ps_bc[0:128, :], one_row[:, :], linv,
                         start=True, stop=True,
                         wait=[DV.at()] + rel.claim(("ps", "bc"), "tensor"),
                         inc=[PE.inc()])
                    p.op("scalar", "activation", bcl, ps_bc[0:128, :], AF.Copy,
                         wait=[PE.at()] + rel.claim(("bcl",), "scalar"),
                         inc=[ACs.inc()])
                    rel.use(("ps", "bc"), "scalar", ACs)
                    p.op("vector", "tensor_tensor", ctxo[cbank][:, :],
                         ps_ctx[cbank][:, :], bcl, ALU.mult,
                         wait=[ACs.at()] + rel.claim(("ctxo", cbank), "vector"),
                         inc=[DV.inc()])
                    rel.use(("ps", "ctx", cbank), "vector", DV)
                    rel.use(("bcl",), "vector", DV)
                    stc = L[f"SCX{cbank}"]
                    p.op("sync", "dma_start",
                         ctx_s[h * 128:(h + 1) * 128, qn * 512:(qn + 1) * 512],
                         ctxo[cbank][:, :], wait=[DV.at()], inc=[stc.inc(16)])
                    rel.use(("ctxo", cbank), "sync", stc)
                    if qn_hook:
                        qn_hook()
                return f

            pend_a, pend_b = mk_a(), mk_b()
        if pend_a is not None:
            pend_a()
            pend_a = None
        if pend_b is not None:
            pend_b()
            pend_b = None
        rel.use(("ps", "sc", 0), "scalar", ACs)
        rel.use(("ps", "sc", 1), "scalar", ACs)
        rel.use(("ps", "p", 0), "scalar", ACs)
        rel.use(("ps", "p", 1), "scalar", ACs)
        rel.use(("cqb", 1), "tensor", PE)
        if head_hook:
            head_hook(h)
        rel.use(("Qh",), "tensor", PE)
        rel.use(("Vh",), "tensor", PE)
    ctx_done = [L["SCX0"].at(), L["SCX1"].at()]
    rel.use(("ps", "c", 0), "vector", DV)
    rel.use(("ps", "c", 1), "vector", DV)
    rel.use(("ps", "s1"), "vector", DV)
    rel.use(("ps", "s2"), "scalar", ACs)
    rel.use(("ps", "bc"), "scalar", ACs)
    p2_pe = PE.at()

    S = arena
    ctxb = Vh
    wob = [Kc[:, 0:2048], Kc[:, 2048:4096]]
    tmp_s = t["tmp_ln"]
    xres = t["xres"]
    ps_c = [t["ps_c0"], t["ps_c1"]]
    for qn in range(4):
        lcx = L["LCX"]
        w = rel.claim(("ctxb",), "scalar") + (list(ctx_done) if qn == 0 else [])
        if qn == 0:
            w += rel.claim(("Vh",), "scalar") + [p2_pe]
        for ht in range(NH):
            p.op("scalar", "dma_start", ctxb[:, ht * 512:(ht + 1) * 512],
                 ctx_s[ht * 128:(ht + 1) * 128, qn * 512:(qn + 1) * 512],
                 wait=(w if ht == 0 else []), inc=[lcx.inc(16)])
        cld = lcx.at()
        for et in range(NDT):
            lwo = L[f"LWO{et % 2}"]
            w2 = rel.claim(("wob", et % 2), "gpsimd")
            if qn == 0 and et < 2:
                w2 += rel.claim(("Kc",), "gpsimd") + [p2_pe]
            p.op("gpsimd", "dma_start", wob[et % 2], io["woT4"][et], wait=w2,
                 inc=[lwo.inc(16)])
            bank = et % 2
            for ht in range(NH):
                w3 = []
                if ht == 0:
                    w3 = [lwo.at(), cld] + rel.claim(("ps", "c", bank), "tensor")
                p.op("tensor", "matmul", ps_c[bank][:, :],
                     wob[et % 2][:, ht * 128:(ht + 1) * 128],
                     ctxb[:, ht * 512:(ht + 1) * 512],
                     start=(ht == 0), stop=(ht == NH - 1),
                     wait=w3, inc=[PE.inc()] if ht == NH - 1 else [])
            rel.use(("wob", et % 2), "tensor", PE)
            rel.use(("ctxb",), "tensor", PE)
            lx = L[f"LXR{et % 2}"]
            p.op("gpsimd", "dma_start", xres[et % 2],
                 xT[et * 128:(et + 1) * 128, qn * 512:(qn + 1) * 512],
                 wait=rel.claim(("xres", et % 2), "gpsimd"), inc=[lx.inc(16)])
            p.op("vector", "tensor_single_scalar", tmp_s[et % 2], ps_c[bank][:, :],
                 t["bo"][:, et:et + 1], ALU.add,
                 wait=[PE.at()] + rel.claim(("tmp_ln", et % 2), "vector"),
                 inc=[DV.inc()])
            rel.use(("ps", "c", bank), "vector", DV)
            p.op("vector", "tensor_tensor", S[:, et * 512:(et + 1) * 512],
                 tmp_s[et % 2], xres[et % 2], ALU.add,
                 wait=[lx.at()] + rel.claim(("S", et), "vector"),
                 inc=[DV.inc()])
            rel.use(("tmp_ln", et % 2), "vector", DV)
            rel.use(("xres", et % 2), "vector", DV)
        emit_ln_chunk(p, c, t, rel, S, qn, io["out"], "SOB")


def alloc_mla_tensors(nc, es):
    t = {}
    sb = lambda name, shape, dt: es.enter_context(nc.sbuf_tensor(name, shape, dt))
    t["one_col"] = sb("one_col", [128, 1], F32)
    t["one_row"] = sb("one_row", [1, 128], F32)
    t["one_col16"] = sb("one_col16", [128, 1], BF16)
    t["one_row16"] = sb("one_row16", [1, 128], BF16)
    t["crow"] = sb("crow", [1, 8], F32)
    t["wqr"] = sb("wqr", [128, 64], BF16)
    t["wkr"] = sb("wkr", [128, 64], BF16)
    t["bdkv"] = sb("bdkv", [128, 4], F32)
    t["bdq"] = sb("bdq", [128, 12], F32)
    t["buk"] = sb("buk", [128, 16], F32)
    t["buq"] = sb("buq", [128, 16], F32)
    t["bo"] = sb("bo", [128, 16], F32)
    t["buv_h"] = [sb(f"buv_h{i}", [1, 128], BF16) for i in range(2)]
    t["ckv"] = sb("ckv", [128, 4 * SEQ], BF16)
    t["Kc"] = sb("Kc", [128, SEQ], BF16)
    t["khraw"] = [sb(f"khraw{i}", [128, 512], BF16) for i in range(2)]
    t["kcol"] = sb("kcol", [128, 1], F32)
    t["ckvbar"] = sb("ckvbar", [128, 4], BF16)
    t["ckvsum"] = sb("ckvsum", [128, 1], F32)
    t["Vh"] = sb("Vh", [128, SEQ], BF16)
    t["Qh"] = sb("Qh", [128, NTOK], BF16)
    t["arena"] = sb("arena", [128, SEQ], F32)
    t["probs"] = [sb(f"probs{i}", [128, 512], BF16) for i in range(3)]
    t["lp"] = [sb(f"lp{i}", [128, 512], BF16) for i in range(2)]
    t["mq"] = sb("mq", [128, 1], F32)
    t["mk"] = sb("mk", [128, 1], F32)
    t["prod"] = sb("prod", [128, 1], F32)
    t["negc"] = sb("negc", [1, 1], F32)
    t["negc2"] = sb("negc2", [1, 1], F32)
    t["negc_b"] = sb("negc_b", [128, 1], F32)
    t["ctxo"] = [sb(f"ctxo{i}", [128, 512], BF16) for i in range(2)]
    UB = sb("UB", [128, 17408], BF16)
    t["wdqb"] = [UB[:, 0:2048], UB[:, 2048:4096]]
    t["wdkv"] = UB[:, 5120:13312]
    t["cqe"] = [UB[:, 4096:4608], UB[:, 4608:5120]]
    t["wukvb"] = [UB[:, 0:1024], UB[:, 1024:2048]]
    t["wuqb"] = [UB[:, 2048:3584], UB[:, 3584:5120]]
    t["cqb"] = [UB[:, 5120:11264], UB[:, 11264:17408]]
    UF = sb("UF", [128, 7680], F32)
    t["qcos"] = UF[0:32, 0:2048]
    t["qsin"] = UF[32:64, 0:2048]
    t["rope_t"] = [UF[0:32, 2048:2560], UF[0:32, 2560:3072]]
    t["linv"] = UF[0:1, 3584:4096]
    t["bcl"] = UF[:, 4096:4608]
    t["row_m"] = UF[0:1, 0:512]
    t["row_v"] = UF[0:1, 512:1024]
    t["row_r"] = UF[0:1, 1024:1536]
    t["row_r2"] = UF[0:1, 1536:2048]
    t["row_mr"] = UF[0:1, 2048:2560]
    t["ssq"] = [UF[:, 2560:3072], UF[:, 3072:3584]]
    t["xres"] = [UF[:, 4608:5120], UF[:, 5120:5632]]
    t["bc_r"] = UF[:, 5632:6144]
    t["bc_mr"] = UF[:, 6144:6656]
    t["tmp_ln"] = [UF[:, 6656:7168], UF[:, 7168:7680]]
    t["obuf"] = [UF[:, 3584:4096], UF[:, 4096:4608]]
    t["xbf"] = [t["Kc"][:, :], t["Vh"][:, :]]
    ps = lambda name, shape: es.enter_context(nc.psum_tensor(name, shape, F32))
    t["ps_sc0"] = ps("ps_sc0", [128, 512])
    t["ps_sc1"] = ps("ps_sc1", [128, 512])
    t["ps_ctx0"] = ps("ps_ctx0", [128, 512])
    t["ps_ctx1"] = ps("ps_ctx1", [128, 512])
    t["ps_p0"] = ps("ps_p0", [128, 512])
    t["ps_p1"] = ps("ps_p1", [128, 512])
    t["ps_l"] = ps("ps_l", [1, 512])
    t["ps_bc"] = ps("ps_bc", [128, 512])
    t["ps_c0"] = t["ps_ctx0"]
    t["ps_c1"] = t["ps_ctx1"]
    t["ps_s1"] = t["ps_l"]
    t["ps_s2"] = t["ps_p0"][0:1, :]
    t["ctx_s"] = nc.dram_tensor("ctx_s", [DM, NTOK], BF16, kind="Internal").ap()
    t["cq_s"] = nc.dram_tensor("cq_s", [12, 128, NTOK], BF16, kind="Internal").ap()
    return t


MLA_LANES = (["MISC", "LCX"] + [f"LX{i}" for i in range(2)] + [f"LUQ{i}" for i in range(2)]
             + [f"LUK{i}" for i in range(2)] + [f"LDQ{i}" for i in range(2)]
             + [f"SCX{i}" for i in range(2)] + [f"LWO{i}" for i in range(2)]
             + [f"LXR{i}" for i in range(2)] + [f"SOB{i}" for i in range(2)]
             + [f"SCQ{i}" for i in range(2)] + [f"LCQ{i}" for i in range(2)]
             + [f"LBV{i}" for i in range(2)])


def build_mla_stage():
    nc = bass.Bass("TRN2", target_bir_lowering=False, debug=False)
    io = {}
    io["xT"] = nc.dram_tensor("xT", [DM, SEQ], F32, kind="ExternalInput").ap()
    io["wdkv_in"] = nc.dram_tensor("wdkv_in", [DM, KVC], F32, kind="ExternalInput").ap()
    io["wdqT4"] = nc.dram_tensor("wdqT4", [12, 128, NDT * 128], F32, kind="ExternalInput").ap()
    io["wuqT4"] = nc.dram_tensor("wuqT4", [NH, 128, 12 * 128], F32, kind="ExternalInput").ap()
    io["wukvT4"] = nc.dram_tensor("wukvT4", [NH, 128, 8 * 128], F32, kind="ExternalInput").ap()
    io["wqr_in"] = nc.dram_tensor("wqr_in", [128, 64], F32, kind="ExternalInput").ap()
    io["wkr_in"] = nc.dram_tensor("wkr_in", [128, 64], F32, kind="ExternalInput").ap()
    io["woT4"] = nc.dram_tensor("woT4", [NDT, 128, 16 * 128], F32, kind="ExternalInput").ap()
    io["bdkv_in"] = nc.dram_tensor("bdkv_in", [128, 4], F32, kind="ExternalInput").ap()
    io["bdq_in"] = nc.dram_tensor("bdq_in", [128, 12], F32, kind="ExternalInput").ap()
    io["buk_in"] = nc.dram_tensor("buk_in", [128, 16], F32, kind="ExternalInput").ap()
    io["buq_in"] = nc.dram_tensor("buq_in", [128, 16], F32, kind="ExternalInput").ap()
    io["bo_in"] = nc.dram_tensor("bo_in", [128, 16], F32, kind="ExternalInput").ap()
    io["buv_in"] = nc.dram_tensor("buv_in", [1, DM], F32, kind="ExternalInput").ap()
    io["ropek_in"] = nc.dram_tensor("ropek_in", [64, SEQ], F32, kind="ExternalInput").ap()
    io["ropeq_in"] = nc.dram_tensor("ropeq_in", [64, NTOK], F32, kind="ExternalInput").ap()
    io["ones_col_in"] = nc.dram_tensor("ones_col_in", [128, 1], F32, kind="ExternalInput").ap()
    io["ones_row_in"] = nc.dram_tensor("ones_row_in", [1, 128], F32, kind="ExternalInput").ap()
    io["consts_in"] = nc.dram_tensor("consts_in", [1, 8], F32, kind="ExternalInput").ap()
    io["out"] = nc.dram_tensor("r1T", [DM, NTOK], F32, kind="ExternalOutput").ap()
    with ExitStack() as es:
        t = alloc_mla_tensors(nc, es)
        c = {k: Ctr(es, nc, k) for k in ("PE", "AC", "DV")}
        c["lanes"] = {k: Ctr(es, nc, k) for k in MLA_LANES}
        p = Prog(nc)
        p.chains = {e: Ctr(es, nc, "CH_" + e) for e in ("vector", "scalar", "tensor", "gpsimd")}
        M = c["lanes"]["MISC"]

        def mload(dst, src, eng="gpsimd"):
            p.op(eng, "dma_start", dst, src, inc=[M.inc(16)])
        mload(t["one_col"][:, :], io["ones_col_in"], "sync")
        mload(t["one_row"][:, :], io["ones_row_in"], "sync")
        mload(t["crow"][:, :], io["consts_in"], "sync")
        mload(t["one_col16"][:, :], io["ones_col_in"])
        mload(t["one_row16"][:, :], io["ones_row_in"])
        for dt in range(NDT):
            mload(t["wdkv"][:, dt * 512:(dt + 1) * 512],
                  io["wdkv_in"][dt * 128:(dt + 1) * 128, :])
        mload(t["wqr"][:, :], io["wqr_in"])
        mload(t["wkr"][:, :], io["wkr_in"])
        for nm in ("bdkv", "bdq", "buk", "buq", "bo"):
            mload(t[nm][:, :], io[nm + "_in"], "sync")
        mload(t["arena"][0:64, :], io["ropek_in"], "sync")
        mload(t["qcos"], io["ropeq_in"][0:32, :], "sync")
        mload(t["qsin"], io["ropeq_in"][32:64, :], "sync")
        emit_mla(nc, p, c, t, io)
        for k in MLA_LANES:
            if k.startswith("S"):
                p.wait("sync", *c["lanes"][k].at())
        p.finish()
    return nc



NKV = 4
EXT = NTOK + 512
NET = EXT // 512
NTT = EXT // 128


def emit_gqa(nc, p, c, t, io, head_hook=None, qn_hook=None):
    PE, ACs, DV = c["PE"], c["AC"], c["DV"]
    L = c["lanes"]
    rel = Rel()
    one_col, one_row = t["one_col"], t["one_row"]
    r2T, r2nT = io["r2T"], io.get("r2nT")
    Kg, Vg, Qh = t["Kg"], t["Vg"], t["Qh"]
    arena = t["arena"]
    kcos, ksin = arena[0:64, 0:EXT], arena[64:128, 0:EXT]
    qcos, qsin = arena[0:64, 2560:4608], arena[64:128, 2560:4608]
    wqb, wkvb, xbf = t["wqb"], t["wkvb"], t["xbf_c"]
    maskb = t["maskb"]
    probs, probm, lp = t["probs"], t["probm"], t["lp"]
    rope_t = t["rope_c"]
    mq, mk, prod = t["mq"], t["mk"], t["prod"]
    negc, negc_b = t["negc"], t["negc_b"]
    bcl, linv = t["bcl"], t["linv"]
    ctxo = t["ctxo"]
    one_col16 = t["one_col16"]
    ps_sc = [t["ps_sc0"], t["ps_sc1"]]
    ps_ctx = [t["ps_ctx0"], t["ps_ctx1"]]
    ps_p = [t["ps_p0"], t["ps_p1"]]
    ps_l, ps_bc = t["ps_l"], t["ps_bc"]
    ctx_s, r2b_s = t["ctx_s"], t["r2b_s"]
    misc_ready = L["MISC"].at()
    LPE, LPC = "vector", DV

    for g in range(NKV):
        lkv = L[f"LKV{g % 2}"]
        p.op("gpsimd", "dma_start", wkvb[g], io["wkvT4"][g],
             wait=([misc_ready] if g == 0 else []), inc=[lkv.inc(16)])
    wkv_ld = [L["LKV0"].at(), L["LKV1"].at()]
    for ch in range(NET):
        ln = L["LX0"]
        w = rel.claim(("xbf", 0), "gpsimd")
        if ch == 0:
            w.append(misc_ready)
        if ch == 0 and io.get("nbr") is not None:
            coutap, sidx, xg32 = io["nbr"]
            for et in range(NDT):
                p.op("gpsimd", "indirect_dma_start", xg32, None, coutap,
                     IndirectOffsetOnAxis(sidx[:, et:et + 1], 0),
                     wait=(w if et == 0 else [c["DV"].at()]), inc=[ln.inc(16)])
                p.op("vector", "tensor_copy", xbf[0][:, et * 512:(et + 1) * 512],
                     xg32, wait=[ln.at()], inc=[DV.inc()])
            ld = DV.at()
        else:
            for et in range(NDT):
                src = (r2nT[et * 128:(et + 1) * 128, :] if ch == 0 else
                       r2T[et * 128:(et + 1) * 128, (ch - 1) * 512: ch * 512])
                p.op("gpsimd", "dma_start", xbf[0][:, et * 512:(et + 1) * 512],
                     src, wait=(w if et == 0 else []), inc=[ln.inc(16)])
            ld = ln.at()
        if ch >= 1:
            so = L[f"SRB{ch % 2}"]
            p.op("sync", "dma_start", r2b_s[ch - 1], xbf[0][:, :],
                 wait=[ld], inc=[so.inc(16)])
            rel.use(("xbf", 0), "sync", so)
        cs = slice(ch * 512, (ch + 1) * 512)
        for g in range(NKV):
            bank = g % 2
            for et in range(NDT):
                w2 = []
                if et == 0:
                    w2 = [ld] + wkv_ld + rel.claim(("ps", "p", bank), "tensor")
                p.op("tensor", "matmul", ps_p[bank][:, :],
                     wkvb[g][:, et * 128:(et + 1) * 128],
                     xbf[0][:, et * 512:(et + 1) * 512],
                     start=(et == 0), stop=(et == NDT - 1),
                     wait=w2, inc=[PE.inc()] if et == NDT - 1 else [])
            p.op("vector", "tensor_tensor", rope_t[0], ps_p[bank][0:64, :],
                 kcos[:, cs], ALU.mult,
                 wait=[PE.at()] + rel.claim(("rope_c", 0), "vector"))
            p.op("vector", "tensor_tensor", rope_t[1], ps_p[bank][64:128, :],
                 ksin[:, cs], ALU.mult,
                 wait=rel.claim(("rope_c", 1), "vector"))
            p.op("vector", "tensor_tensor", Kg[g][0:64, cs],
                 rope_t[0], rope_t[1], ALU.subtract, inc=[DV.inc()])
            p.op("vector", "tensor_tensor", rope_t[2], ps_p[bank][64:128, :],
                 kcos[:, cs], ALU.mult,
                 wait=rel.claim(("rope_c", 2), "vector"))
            p.op("vector", "tensor_tensor", rope_t[3], ps_p[bank][0:64, :],
                 ksin[:, cs], ALU.mult,
                 wait=rel.claim(("rope_c", 3), "vector"))
            p.op("vector", "tensor_tensor", Kg[g][64:128, cs],
                 rope_t[2], rope_t[3], ALU.add, inc=[DV.inc()])
            rel.use(("ps", "p", bank), "vector", DV)
            for i in range(4):
                rel.use(("rope_c", i), "vector", DV)
            vbank = g % 2
            wv0 = rel.claim(("ps", "sc", vbank), "tensor")
            for sub in range(4):
                for et in range(NDT):
                    w2 = (wv0 + [ld] + wkv_ld) if (sub == 0 and et == 0) else []
                    p.op("tensor", "matmul", ps_sc[vbank][:, sub * 128:(sub + 1) * 128],
                         xbf[0][:, et * 512 + sub * 128: et * 512 + (sub + 1) * 128],
                         wkvb[g][:, 2048 + et * 128: 2048 + (et + 1) * 128],
                         start=(et == 0), stop=(et == NDT - 1),
                         wait=w2, inc=[PE.inc()] if (sub == 3 and et == NDT - 1) else [])
            p.op("scalar", "activation", Vg[g][:, cs],
                 ps_sc[vbank][:, :], AF.Copy, wait=[PE.at()], inc=[ACs.inc()])
            rel.use(("ps", "sc", vbank), "scalar", ACs)
        rel.use(("xbf", 0), "tensor", PE)
    kv_pe = PE.at()
    kv_dv = DV.at()
    kv_ac = ACs.at()
    r2b_done = [L["SRB0"].at(), L["SRB1"].at()]

    for h in range(NH):
        g = h // 4
        lnq = L[f"LUQ{h % 2}"]
        wqw = rel.claim(("wqb", h % 2), "gpsimd")
        if h < 2:
            wqw.append(kv_pe)
        p.op("gpsimd", "dma_start", wqb[h % 2], io["wqT4"][h], wait=wqw,
             inc=[lnq.inc(16)])
        for qn in range(4):
            bank = qn % 2
            lcq = L[f"LCQ{qn % 2}"]
            wq0 = rel.claim(("cqb", qn % 2), "sync")
            if h == 0 and qn < 2:
                wq0 += list(r2b_done) + [kv_pe]
            p.op("sync", "dma_start", t["cqb"][qn % 2][:, :], r2b_s[qn],
                 wait=wq0, inc=[lcq.inc(16)])
            for et in range(NDT):
                w2 = []
                if et == 0:
                    w2 = [lnq.at(), lcq.at()] + rel.claim(("ps", "p", bank), "tensor")
                    if h == 0 and qn < 2:
                        w2.append(kv_dv)
                p.op("tensor", "matmul", ps_p[bank][:, :],
                     wqb[h % 2][:, et * 128:(et + 1) * 128],
                     t["cqb"][qn % 2][:, et * 512:(et + 1) * 512],
                     start=(et == 0), stop=(et == NDT - 1),
                     wait=w2, inc=[PE.inc()] if et == NDT - 1 else [])
            rel.use(("cqb", qn % 2), "tensor", PE)
            qs = slice(qn * 512, (qn + 1) * 512)
            p.op("vector", "tensor_tensor", rope_t[0], ps_p[bank][0:64, :],
                 qcos[:, qs], ALU.mult,
                 wait=[PE.at()] + rel.claim(("rope_c", 0), "vector"))
            p.op("vector", "tensor_tensor", rope_t[1], ps_p[bank][64:128, :],
                 qsin[:, qs], ALU.mult,
                 wait=rel.claim(("rope_c", 1), "vector"))
            p.op("vector", "tensor_tensor", Qh[0:64, qs],
                 rope_t[0], rope_t[1], ALU.subtract,
                 wait=(rel.claim(("Qh",), "vector") if qn == 0 else []),
                 inc=[DV.inc()])
            p.op("vector", "tensor_tensor", rope_t[2], ps_p[bank][64:128, :],
                 qcos[:, qs], ALU.mult,
                 wait=rel.claim(("rope_c", 2), "vector"))
            p.op("vector", "tensor_tensor", rope_t[3], ps_p[bank][0:64, :],
                 qsin[:, qs], ALU.mult,
                 wait=rel.claim(("rope_c", 3), "vector"))
            p.op("vector", "tensor_tensor", Qh[64:128, qs],
                 rope_t[2], rope_t[3], ALU.add, inc=[DV.inc()])
            rel.use(("ps", "p", bank), "vector", DV)
            for i in range(4):
                rel.use(("rope_c", i), "vector", DV)
        qh_dv = DV.at()
        rel.use(("wqb", h % 2), "tensor", PE)

        p.wait("vector", *qh_dv)
        if h == 0:
            p.wait("vector", *kv_dv)
        p.top("vector", "tensor_reduce", mq[:, :], Qh[:, :], AX.X, ALU.max,
              apply_absolute_value=True)
        p.top("vector", "tensor_reduce", mk[:, :], Kg[g][:, :], AX.X, ALU.max,
              apply_absolute_value=True)
        p.top("vector", "tensor_tensor", prod[:, :], mq[:, :], mk[:, :], ALU.mult,
              inc=[DV.inc()])
        p.op("tensor", "matmul", ps_l[0:1, 0:1], one_col[:, :], prod[:, :],
             start=True, stop=True, wait=[DV.at()] + rel.claim(("ps", "l"), "tensor"),
             inc=[PE.inc()])
        p.top("vector", "tensor_scalar", negc[:, :], ps_l[0:1, 0:1],
              -SCALE_QK, 20.0, ALU.mult, ALU.add, wait=[PE.at()])
        p.top("vector", "tensor_single_scalar", negc[:, :] if False else t["negc2"][:, :],
              negc[:, :], -55.0, ALU.max, inc=[DV.inc()])
        rel.use(("ps", "l"), "vector", DV)
        p.op("tensor", "matmul", ps_bc[0:128, 0:1], one_row[:, :], t["negc2"][:, :],
             start=True, stop=True, wait=[DV.at()] + rel.claim(("ps", "bc"), "tensor"),
             inc=[PE.inc()])
        p.top("scalar", "activation", negc_b[:, :], ps_bc[0:128, 0:1], AF.Copy,
              wait=[PE.at()], inc=[ACs.inc()])
        rel.use(("ps", "bc"), "scalar", ACs)
        negc_ac = ACs.at()

        if h == 0:
            lmk = L["LMK0"]
            for qb in range(4):
                p.op("sync", "dma_start", maskb[qb][:, :], io["masks"][qb],
                     inc=[lmk.inc(16)])
            mask_ld = L["LMK0"].at()
        pend_a = pend_b = None
        for qb in range(4):
            cbank = qb % 2
            pv_w0 = rel.claim(("ps", "ctx", cbank), "tensor")
            sc4 = (ps_sc[0], ps_sc[1], ps_p[0], ps_p[1])
            key4 = (("ps", "sc", 0), ("ps", "sc", 1), ("ps", "p", 0), ("ps", "p", 1))
            qk_done = {}

            def emit_qk(j, qb=qb):
                kt = 4 * qb + j
                b = j % 4
                w2 = rel.claim(key4[b], "tensor")
                if h == 0 and qb == 0 and j == 0:
                    w2 += [kv_dv, kv_ac]
                p.op("tensor", "matmul", sc4[b][:, :],
                     Kg[g][:, kt * 128:(kt + 1) * 128], Qh[:, qb * 512:(qb + 1) * 512],
                     start=True, stop=True, wait=w2, inc=[PE.inc()])
                qk_done[j] = PE.at()

            for j in (0, 1, 2):
                emit_qk(j)
            if pend_a is not None:
                pend_a()
                pend_a = None
            for j in range(8):
                if j == 5 and pend_b is not None:
                    pend_b()
                    pend_b = None
                kt = 4 * qb + j
                b = j % 4
                pslot = j % 2
                we = [qk_done[j]] + rel.claim(("probs", pslot), "scalar")
                if qb == 0 and j == 0:
                    we.append(negc_ac)
                p.op("scalar", "activation", probs[pslot][:, :], sc4[b][:, :],
                     AF.Exp, bias=negc_b[:, 0:1], scale=SCALE_QK,
                     wait=we, inc=[ACs.inc()])
                rel.use(key4[b], "scalar", ACs)
                p.op(LPE, "tensor_tensor", probm[pslot][:, :], probs[pslot][:, :],
                     maskb[qb][:, j * 512:(j + 1) * 512], ALU.mult,
                     wait=[ACs.at(), mask_ld] + rel.claim(("probm", pslot), LPE),
                     inc=[LPC.inc()])
                rel.use(("probs", pslot), LPE, LPC)
                if j == 0:
                    p.op(LPE, "tensor_copy", lp[1][:, :], probm[pslot][:, :],
                         wait=rel.claim(("lp", 1), LPE), inc=[LPC.inc()])
                else:
                    p.op(LPE, "tensor_tensor", lp[(j + 1) % 2][:, :],
                         lp[j % 2][:, :], probm[pslot][:, :], ALU.add,
                         wait=rel.claim(("lp", (j + 1) % 2), LPE), inc=[LPC.inc()])
                w3 = [LPC.at()]
                if j == 0:
                    w3 += pv_w0
                p.op("tensor", "matmul", ps_ctx[cbank][:, :],
                     Vg[g][:, kt * 128:(kt + 1) * 128], probm[pslot][:, :],
                     start=(j == 0), stop=(j == 7),
                     wait=w3, inc=[PE.inc()])
                rel.use(("probm", pslot), "tensor", PE)
                if j + 3 < 8:
                    emit_qk(j + 3)
            def mk_a():
                def f():
                    p.op("tensor", "matmul", ps_l[0:1, :], one_col16[:, :],
                         lp[0][:, :], start=True, stop=True,
                         wait=[LPC.at()] + rel.claim(("ps", "l"), "tensor"),
                         inc=[PE.inc()])
                    rel.use(("lp", 0), "tensor", PE)
                    p.top("vector", "reciprocal", linv, ps_l[0:1, :],
                          wait=[PE.at()], inc=[DV.inc()])
                    rel.use(("ps", "l"), "vector", DV)
                return f

            def mk_b(qb=qb, cbank=cbank):
                def f():
                    p.op("tensor", "matmul", ps_bc[0:128, :], one_row[:, :], linv,
                         start=True, stop=True,
                         wait=[DV.at()] + rel.claim(("ps", "bc"), "tensor"),
                         inc=[PE.inc()])
                    p.op("scalar", "activation", bcl, ps_bc[0:128, :], AF.Copy,
                         wait=[PE.at()] + rel.claim(("bcl",), "scalar"),
                         inc=[ACs.inc()])
                    rel.use(("ps", "bc"), "scalar", ACs)
                    p.op("vector", "tensor_tensor", ctxo[cbank][:, :],
                         ps_ctx[cbank][:, :], bcl, ALU.mult,
                         wait=[ACs.at()] + rel.claim(("ctxo", cbank), "vector"),
                         inc=[DV.inc()])
                    rel.use(("ps", "ctx", cbank), "vector", DV)
                    rel.use(("bcl",), "vector", DV)
                    stc = L[f"SCX{cbank}"]
                    p.op("sync", "dma_start",
                         ctx_s[h * 128:(h + 1) * 128, qb * 512:(qb + 1) * 512],
                         ctxo[cbank][:, :], wait=[DV.at()], inc=[stc.inc(16)])
                    rel.use(("ctxo", cbank), "sync", stc)
                    if qn_hook:
                        qn_hook()
                return f

            pend_a, pend_b = mk_a(), mk_b()
        if pend_a is not None:
            pend_a()
            pend_a = None
        if pend_b is not None:
            pend_b()
            pend_b = None
        rel.use(("Qh",), "tensor", PE)
        if head_hook:
            head_hook(h)
    ctx_done = [L["SCX0"].at(), L["SCX1"].at()]
    rel.use(("ps", "c", 0), "vector", DV)
    rel.use(("ps", "c", 1), "vector", DV)
    rel.use(("ps", "s1"), "vector", DV)
    rel.use(("ps", "s2"), "vector", DV)
    rel.use(("ps", "bc"), "scalar", ACs)
    p2_pe = PE.at()

    S = arena
    ctxb = t["ctxb_c"]
    wob = t["wob_c"]
    tmp_s = t["tmp_ln"]
    xres = t["xres"]
    ps_c = [t["ps_c0"], t["ps_c1"]]
    for qn in range(4):
        lcx = L["LCX"]
        w = rel.claim(("ctxb",), "scalar") + (list(ctx_done) + [p2_pe] if qn == 0 else [])
        for ht in range(NH):
            p.op("scalar", "dma_start", ctxb[:, ht * 512:(ht + 1) * 512],
                 ctx_s[ht * 128:(ht + 1) * 128, qn * 512:(qn + 1) * 512],
                 wait=(w if ht == 0 else []), inc=[lcx.inc(16)])
        cld = lcx.at()
        for et in range(NDT):
            lwo = L[f"LWO{et % 2}"]
            w2 = rel.claim(("wob", et % 2), "gpsimd")
            if qn == 0 and et < 2:
                w2 += [p2_pe]
            p.op("gpsimd", "dma_start", wob[et % 2], io["woT4"][et], wait=w2,
                 inc=[lwo.inc(16)])
            bank = et % 2
            for ht in range(NH):
                w3 = []
                if ht == 0:
                    w3 = [lwo.at(), cld] + rel.claim(("ps", "c", bank), "tensor")
                p.op("tensor", "matmul", ps_c[bank][:, :],
                     wob[et % 2][:, ht * 128:(ht + 1) * 128],
                     ctxb[:, ht * 512:(ht + 1) * 512],
                     start=(ht == 0), stop=(ht == NH - 1),
                     wait=w3, inc=[PE.inc()] if ht == NH - 1 else [])
            rel.use(("wob", et % 2), "tensor", PE)
            rel.use(("ctxb",), "tensor", PE)
            lx = L[f"LXR{et % 2}"]
            p.op("gpsimd", "dma_start", xres[et % 2],
                 r2T[et * 128:(et + 1) * 128, qn * 512:(qn + 1) * 512],
                 wait=rel.claim(("xres", et % 2), "gpsimd"), inc=[lx.inc(16)])
            p.op("vector", "tensor_tensor", S[:, et * 512:(et + 1) * 512],
                 ps_c[bank][:, :], xres[et % 2], ALU.add,
                 wait=[PE.at(), lx.at()] + rel.claim(("S", et), "vector"),
                 inc=[DV.inc()])
            rel.use(("ps", "c", bank), "vector", DV)
            rel.use(("xres", et % 2), "vector", DV)
        emit_ln_chunk(p, c, t, rel, S, qn, io["out"], "SOB")


def alloc_gqa_tensors(nc, es, dbg_ctx=False):
    t = {}
    sb = lambda name, shape, dt: es.enter_context(nc.sbuf_tensor(name, shape, dt))
    t["one_col"] = sb("one_col", [128, 1], F32)
    t["one_row"] = sb("one_row", [1, 128], F32)
    t["one_col16"] = sb("one_col16", [128, 1], BF16)
    t["crow"] = sb("crow", [1, 8], F32)
    t["Kg"] = [sb(f"Kg{i}", [128, EXT], BF16) for i in range(NKV)]
    t["Vg"] = [sb(f"Vg{i}", [128, EXT], BF16) for i in range(NKV)]
    t["Qh"] = sb("Qh", [128, NTOK], BF16)
    t["arena"] = sb("arena", [128, SEQ], F32)
    t["maskb"] = [sb(f"maskb{i}", [128, 8 * 512], BF16) for i in range(2)]
    t["probs"] = [sb(f"probs{i}", [128, 512], BF16) for i in range(2)]
    t["probm"] = [sb(f"probm{i}", [128, 512], BF16) for i in range(2)]
    t["lp"] = [sb(f"lp{i}", [128, 512], BF16) for i in range(2)]
    t["mq"] = sb("mq", [128, 1], F32)
    t["mk"] = sb("mk", [128, 1], F32)
    t["prod"] = sb("prod", [128, 1], F32)
    t["negc"] = sb("negc", [1, 1], F32)
    t["negc2"] = sb("negc2", [1, 1], F32)
    t["negc_b"] = sb("negc_b", [128, 1], F32)
    t["ctxo"] = [sb(f"ctxo{i}", [128, 512], BF16) for i in range(2)]
    UB = sb("UB", [128, 24576], BF16)
    t["wkvb"] = [UB[:, g * 4096:(g + 1) * 4096] for g in range(4)]
    t["xbf_c"] = [UB[:, 16384:24576]]
    t["wqb"] = [UB[:, 0:2048], UB[:, 2048:4096]]
    t["cqb"] = [UB[:, 4096:12288], UB[:, 12288:20480]]
    t["ctxb_c"] = UB[:, 4096:12288]
    t["wob_c"] = [UB[:, 0:2048], UB[:, 2048:4096]]
    UF = sb("UF", [128, 7680], F32)
    t["rope_c"] = [UF[0:64, 2048 + i * 512: 2048 + (i + 1) * 512] for i in range(2)] + \
                  [UF[0:64, 3072 + i * 512: 3072 + (i + 1) * 512] for i in range(2)]
    t["linv"] = UF[0:1, 4608:5120]
    t["bcl"] = UF[:, 5120:5632]
    t["row_m"] = UF[0:1, 0:512]
    t["row_v"] = UF[0:1, 512:1024]
    t["row_r"] = UF[0:1, 1024:1536]
    t["row_r2"] = UF[0:1, 1536:2048]
    t["row_mr"] = UF[0:1, 2048:2560]
    t["ssq"] = [UF[:, 2560:3072], UF[:, 3072:3584]]
    t["xres"] = [UF[:, 3584:4096], UF[:, 4096:4608]]
    t["bc_r"] = UF[:, 5632:6144]
    t["bc_mr"] = UF[:, 6144:6656]
    t["tmp_ln"] = [UF[:, 6656:7168], UF[:, 7168:7680]]
    t["obuf"] = [UF[:, 4608:5120], UF[:, 5120:5632]]
    ps = lambda name, shape: es.enter_context(nc.psum_tensor(name, shape, F32))
    t["ps_sc0"] = ps("ps_sc0", [128, 512])
    t["ps_sc1"] = ps("ps_sc1", [128, 512])
    t["ps_ctx0"] = ps("ps_ctx0", [128, 512])
    t["ps_ctx1"] = ps("ps_ctx1", [128, 512])
    t["ps_p0"] = ps("ps_p0", [128, 512])
    t["ps_p1"] = ps("ps_p1", [128, 512])
    t["ps_l"] = ps("ps_l", [1, 512])
    t["ps_bc"] = ps("ps_bc", [128, 512])
    t["ps_c0"] = t["ps_ctx0"]
    t["ps_c1"] = t["ps_ctx1"]
    t["ps_s1"] = t["ps_l"]
    t["ps_s2"] = t["ps_p0"][0:1, :]
    t["ctx_s"] = nc.dram_tensor("ctx_s", [DM, NTOK], BF16,
                                kind="ExternalOutput" if dbg_ctx else "Internal").ap()
    t["r2b_s"] = nc.dram_tensor("r2b_s", [4, 128, NDT * 512], BF16, kind="Internal").ap()
    return t


GQA_LANES = (["MISC", "LCX"] + [f"LX{i}" for i in range(2)] + [f"LUQ{i}" for i in range(2)]
             + [f"LKV{i}" for i in range(2)] + [f"LMK{i}" for i in range(2)]
             + [f"SCX{i}" for i in range(2)] + [f"LWO{i}" for i in range(2)]
             + [f"LXR{i}" for i in range(2)] + [f"SOB{i}" for i in range(2)]
             + [f"SRB{i}" for i in range(2)] + [f"LCQ{i}" for i in range(2)])


def build_gqa_stage(debug=False, dbg_ctx=False):
    nc = bass.Bass("TRN2", target_bir_lowering=False, debug=False)
    io = {}
    io["r2T"] = nc.dram_tensor("r2T", [DM, NTOK], F32, kind="ExternalInput").ap()
    io["r2nT"] = nc.dram_tensor("r2nT", [DM, 512], F32, kind="ExternalInput").ap()
    io["wqT4"] = nc.dram_tensor("wqT4", [NH, 128, NDT * 128], F32, kind="ExternalInput").ap()
    io["wkvT4"] = nc.dram_tensor("wkvT4", [NKV, 128, 2 * NDT * 128], F32, kind="ExternalInput").ap()
    io["woT4"] = nc.dram_tensor("woT4", [NDT, 128, 16 * 128], F32, kind="ExternalInput").ap()
    io["ropek_c"] = nc.dram_tensor("ropek_c", [128, EXT], F32, kind="ExternalInput").ap()
    io["ropeq_c"] = nc.dram_tensor("ropeq_c", [128, NTOK], F32, kind="ExternalInput").ap()
    io["masks"] = nc.dram_tensor("masks", [4, 128, 8 * 512], BF16, kind="ExternalInput").ap()
    io["ones_col_in"] = nc.dram_tensor("ones_col_in", [128, 1], F32, kind="ExternalInput").ap()
    io["ones_row_in"] = nc.dram_tensor("ones_row_in", [1, 128], F32, kind="ExternalInput").ap()
    io["consts_in"] = nc.dram_tensor("consts_in", [1, 8], F32, kind="ExternalInput").ap()
    io["out"] = nc.dram_tensor("r3T", [DM, NTOK], F32, kind="ExternalOutput").ap()
    with ExitStack() as es:
        t = alloc_gqa_tensors(nc, es, dbg_ctx)
        c = {k: Ctr(es, nc, k) for k in ("PE", "AC", "DV")}
        c["lanes"] = {k: Ctr(es, nc, k) for k in GQA_LANES}
        p = Prog(nc)
        p.chains = {e: Ctr(es, nc, "CH_" + e) for e in ("vector", "scalar", "tensor", "gpsimd")}
        M = c["lanes"]["MISC"]

        def mload(dst, src, eng="gpsimd"):
            p.op(eng, "dma_start", dst, src, inc=[M.inc(16)])
        mload(t["one_col"][:, :], io["ones_col_in"], "sync")
        mload(t["one_row"][:, :], io["ones_row_in"], "sync")
        mload(t["crow"][:, :], io["consts_in"], "sync")
        mload(t["one_col16"][:, :], io["ones_col_in"])
        mload(t["arena"][0:128, 0:EXT], io["ropek_c"], "sync")
        mload(t["arena"][0:128, 2560:4608], io["ropeq_c"], "sync")
        emit_gqa(nc, p, c, t, io)
        if debug:
            SOB = c["lanes"]["SOB0"]
            w = [c["lanes"][f"SOB{i}"].at() for i in range(2)]
            for nm, ap in [("dbg_kg0", t["Kg"][0][:, :]), ("dbg_vg0", t["Vg"][0][:, :]),
                           ("dbg_qh", t["Qh"][:, :]), ("dbg_negcb", t["negc_b"][:, :])]:
                dt_ = ap.tensor.dtype if hasattr(ap, "tensor") else BF16
                sh = [ap.shape[0], ap.shape[1]]
                o = nc.dram_tensor(nm, sh, dt_, kind="ExternalOutput").ap()
                p.op("sync", "dma_start", o, ap, wait=w, inc=[SOB.inc(16)])
                w = []
        for k in GQA_LANES:
            if k.startswith("S"):
                p.wait("sync", *c["lanes"][k].at())
        p.finish()
    return nc



class GammaPre:

    NG = 33

    def __init__(self, nc, p, c, t, gslab, w3L, thr_b, nthr_b, w3q_s,
                 g_in, g_out):
        self.p, self.c, self.t = p, c, t
        self.thr_b, self.nthr_b, self.w3q_s = thr_b, nthr_b, w3q_s
        self.gslab = gslab
        self.g_in, self.g_out = g_in, g_out
        self.rel = Rel()
        self.w3L = w3L
        self.gi = 0
        self.qi = 0
        self.hook_i = 0
        self.mid_done = False
        self.thr_done = False
        self.ready = None
        self.done = None

    def step(self, n_gamma=24, n_quant=12):
        p, c, t, rel = self.p, self.c, self.t, self.rel
        self.hook_i += 1
        PE, ACs, DV = c["PE"], c["AC"], c["DV"]
        L = c["lanes"]
        gbuf, gpart = t["gbuf_pre"], t["gpartP"]
        ps_l, ps_bc = t["ps_l"], t["ps_bc"]
        one_col, one_row = t["one_col"], t["one_row"]
        k = 0
        while self.gi < self.NG and k < n_gamma:
            j = self.gi
            ln = L[f"LG{j % 2}"]
            p.op("scalar", "dma_start", gbuf[j % 2][:, 0:1024],
                 self.gslab[:, j * 1024:(j + 1) * 1024],
                 wait=rel.claim(("gb", j % 2), "scalar"), inc=[ln.inc(16)])
            p.op("vector", "tensor_reduce", gpart[:, j:j + 1], gbuf[j % 2][:, 0:1024],
                 AX.X, ALU.add, apply_absolute_value=True,
                 wait=[ln.at()], inc=[DV.inc()])
            rel.use(("gb", j % 2), "vector", DV)
            self.gi += 1
            k += 1
        if self.gi == self.NG and not self.mid_done:
            self.mid_done = True
            GIN, CCG = L["GIN"], L["CCG"]
            p.op("tensor", "matmul", ps_l[0:1, 0:33], one_col[:, :],
                 gpart[:, 0:33], start=True, stop=True,
                 wait=[DV.at()] + rel.claim(("ps", "l"), "tensor"), inc=[PE.inc()])
            sg = gpart[0:1, 0:33]
            p.top("scalar", "activation", sg, ps_l[0:1, 0:33], AF.Copy,
                  wait=[PE.at()], inc=[ACs.inc()])
            rel.use(("ps", "l"), "scalar", ACs)
            p.wait("vector", *ACs.at())
            for m in range(3):
                p.top("vector", "tensor_reduce", gpart[0:1, 40 + m:41 + m],
                      sg[0:1, m * 11:(m + 1) * 11], AX.X, ALU.add,
                      inc=[DV.inc()] if m == 2 else [])
            p.op("sync", "dma_start", self.g_in.ap()[:, :], gpart[0:1, 40:44],
                 wait=[DV.at()], inc=[GIN.inc(16)])
            p.op("gpsimd", "collective_compute", "AllReduce", ALU.add,
                 replica_groups=[list(range(8))], ins=[self.g_in.ap().opt()],
                 outs=[self.g_out.ap().opt()], wait=[GIN.at()], inc=[CCG.inc(1)])
            return
        if self.mid_done and not self.thr_done and self.hook_i >= 22:
            self.thr_done = True
            CCG, LGR = L["CCG"], L["LGR"]
            p.op("gpsimd", "dma_start", gpart[0:1, 44:48], self.g_out.ap()[:, :],
                 wait=[CCG.at()], inc=[LGR.inc(16)])
            p.top("vector", "tensor_scalar", gpart[0:1, 48:51], gpart[0:1, 44:47],
                  0.5 / (DM * HFF), 0.5 * EPS_Q, ALU.mult, ALU.add,
                  wait=[LGR.at()], inc=[DV.inc()])
            p.op("tensor", "matmul", ps_bc[0:128, 0:3], one_row[:, :],
                 gpart[0:1, 48:51], start=True, stop=True,
                 wait=[DV.at()] + rel.claim(("ps", "bc"), "tensor"), inc=[PE.inc()])
            p.top("scalar", "activation", self.thr_b[:, 0:3], ps_bc[0:128, 0:3],
                  AF.Copy, wait=[PE.at()], inc=[ACs.inc()])
            rel.use(("ps", "bc"), "scalar", ACs)
            p.top("vector", "tensor_single_scalar", self.nthr_b[:, 0:3],
                  self.thr_b[:, 0:3], -1.0, ALU.mult, wait=[ACs.at()], inc=[DV.inc()])
            self.ready = DV.at()
            return
        if not self.thr_done:
            return
        wq3p, qa_p = t["wq3p"], t["qa_p"]
        k = 0
        while self.qi < 128 and k < n_quant:
            j = self.qi
            ln = L[f"LG{j % 2}"]
            stl = L["SW3P0"]
            p.op("scalar", "dma_start", gbuf[j % 2][:, 0:704],
                 self.w3L[j // 8][:, (j % 8) * 704:(j % 8 + 1) * 704],
                 wait=rel.claim(("gb", j % 2), "scalar"), inc=[ln.inc(16)])
            p.op("vector", "tensor_single_scalar", qa_p[:, :],
                 gbuf[j % 2][:, 0:704], self.nthr_b[:, 2:3], ALU.is_lt,
                 wait=[ln.at()] + ([self.ready] if j < 2 else []))
            p.op("vector", "scalar_tensor_tensor", wq3p[:, :], gbuf[j % 2][:, 0:704],
                 self.thr_b[:, 2:3], qa_p[:, :], ALU.is_gt, ALU.subtract,
                 wait=rel.claim(("wq3p",), "vector"), inc=[DV.inc()])
            rel.use(("gb", j % 2), "vector", DV)
            p.op("sync", "dma_start",
                 self.w3q_s[j // 8][:, (j % 8) * 704:(j % 8 + 1) * 704],
                 wq3p[:, :], wait=[DV.at()], inc=[stl.inc(16)])
            rel.use(("wq3p",), "sync", stl)
            self.qi += 1
            k += 1

    def finish(self):
        self.hook_i = 9999
        while self.gi < self.NG or not self.mid_done or not self.thr_done \
                or self.qi < 128:
            self.step(40, 20)
        L = self.c["lanes"]
        self.done = {"thr_b": self.thr_b, "nthr_b": self.nthr_b,
                     "ready": self.ready, "w3q_done": [L["SW3P0"].at()]}
        return self.done


class Slab:

    def __init__(self, tensor, cols):
        self.t = tensor
        self.cols = cols
        self.cur = 0

    def take(self, cols):
        assert self.cur + cols <= self.cols, f"slab overflow {self.cur}+{cols}>{self.cols}"
        v = self.t[:, self.cur:self.cur + cols]
        self.cur += cols
        return v

    def reset(self):
        self.cur = 0


def _mk_ps(nc, es):
    ps = lambda name, shape: es.enter_context(nc.psum_tensor(name, shape, F32))
    PA = ps("PA", [128, 1024])
    PB = ps("PB", [128, 1024])
    PC0 = ps("PC0", [128, 512])
    PC1 = ps("PC1", [128, 512])
    PL = ps("PL", [1, 512])
    PBC = ps("PBC", [128, 512])
    return (PA, PB, PC0, PC1), PL, PBC


def _ps_attn(t, P, PL, P7):
    PA, PB, PC0, PC1 = P
    t["ps_pair"] = [PA, PB]
    t["ps_sc0"], t["ps_sc1"] = PA[:, 0:512], PB[:, 0:512]
    t["ps_ctx0"], t["ps_ctx1"] = PC0, PC1
    t["ps_p0"], t["ps_p1"] = PA[:, 512:1024], PB[:, 512:1024]
    t["ps_l"], t["ps_bc"] = PL, P7
    t["ps_c0"], t["ps_c1"] = PC0, PC1
    t["ps_s1"] = PL
    t["ps_s2"] = P7[0:1, :]


def _ps_ffn(t, P, PL, P7):
    PA, PB, PC0, PC1 = P
    t["ps_a0"], t["ps_a1"] = PA[:, 0:512], PB[:, 0:512]
    t["ps_b0"], t["ps_b1"] = PA[:, 512:1024], PB[:, 512:1024]
    t["ps_c0"], t["ps_c1"] = PC0, PC1
    t["ps_s1"] = PL
    t["ps_s2"] = P7[0:1, :]


def _ln_views(t, f32):
    t["row_m"] = f32.take(512)[0:1, :]
    t["row_v"] = f32.take(512)[0:1, :]
    t["row_r"] = f32.take(512)[0:1, :]
    t["row_r2"] = f32.take(512)[0:1, :]
    t["row_mr"] = f32.take(512)[0:1, :]
    t["ssq"] = [f32.take(512), f32.take(512)]
    t["xres"] = [f32.take(512), f32.take(512)]
    t["bc_r"] = f32.take(512)
    t["bc_mr"] = f32.take(512)
    t["tmp_ln"] = [f32.take(512), f32.take(512)]
    t["obuf"] = [f32.take(512), f32.take(512)]


def mk_t_mla(shared, b16, f32):
    t = dict(shared)
    t["ckv"] = b16.take(4 * SEQ)
    t["Kc"] = b16.take(SEQ)
    t["Vh"] = b16.take(SEQ)
    t["Qh"] = b16.take(NTOK)
    t["khraw"] = [b16.take(512), b16.take(512)]
    UB = b16.take(13312)
    t["wdqb"] = [UB[:, 0:2048], UB[:, 2048:4096]]
    t["cqe"] = [UB[:, 4096:4608], UB[:, 4608:5120]]
    t["wukvb"] = [UB[:, 0:1024], UB[:, 1024:2048]]
    t["wuqb"] = [UB[:, 2048:3584], UB[:, 3584:5120]]
    t["cqb"] = [UB[:, 5120:11264], t["Vh"][:, 0:6144]]
    t["wdkv"] = UB[:, 5120:13312]
    t["probs"] = [b16.take(1024), b16.take(1024)]
    t["lp"] = [b16.take(1024), b16.take(1024)]
    t["ctxo"] = [b16.take(512), b16.take(512)]
    t["wqr"] = b16.take(64)
    t["wkr"] = b16.take(64)
    t["buv_h"] = [b16.take(128)[0:1, :], b16.take(128)[0:1, :]]
    t["ckvbar"] = b16.take(4)
    t["wq3p"] = b16.take(704)
    t["qa_p"] = b16.take(704)
    arena = f32.take(SEQ)
    t["arena"] = arena
    UF = f32.take(7680)
    t["qcos"] = UF[0:32, 0:2048]
    t["qsin"] = UF[32:64, 0:2048]
    t["rope_t"] = [UF[0:32, 2048:2560], UF[0:32, 2560:3072]]
    t["linv"] = UF[0:1, 3584:4096]
    t["bcl"] = UF[:, 4096:4608]
    t["row_m"] = UF[0:1, 0:512]
    t["row_v"] = UF[0:1, 512:1024]
    t["row_r"] = UF[0:1, 1024:1536]
    t["row_r2"] = UF[0:1, 1536:2048]
    t["row_mr"] = UF[0:1, 2048:2560]
    t["ssq"] = [UF[:, 2560:3072], UF[:, 3072:3584]]
    t["xres"] = [UF[:, 4608:5120], UF[:, 5120:5632]]
    t["bc_r"] = UF[:, 5632:6144]
    t["bc_mr"] = UF[:, 6144:6656]
    t["tmp_ln"] = [UF[:, 6656:7168], UF[:, 7168:7680]]
    t["obuf"] = [UF[:, 3584:4096], UF[:, 4096:4608]]
    t["gbuf_pre"] = [UF[:, 5632:6656], UF[:, 6656:7680]]
    t["bdkv"] = f32.take(4)
    t["bdq"] = f32.take(12)
    t["buk"] = f32.take(16)
    t["buq"] = f32.take(16)
    t["bo"] = f32.take(16)
    t["mq"] = f32.take(1)
    t["mk"] = f32.take(1)
    t["prod"] = f32.take(1)
    t["negc"] = f32.take(1)[0:1, :]
    t["negc2"] = f32.take(1)[0:1, :]
    t["negc_b"] = f32.take(1)
    t["kcol"] = f32.take(1)
    t["ckvsum"] = f32.take(1)
    t["xbf"] = [t["Kc"][:, :], t["Vh"][:, :]]
    return t


def mk_t_ffn(shared, b16, f32):
    t = dict(shared)
    t["rbf"] = b16.take(NDT * 1024)
    t["hid"] = b16.take(NHT * 1024)
    t["qa"] = b16.take(2048)
    t["wq"] = [b16.take(2048) for _ in range(4)]
    t["sg"] = [b16.take(512), b16.take(512)]
    t["S"] = f32.take(NDT * 512)
    t["gsum"] = f32.take(4)[0:1, :]
    t["thr"] = f32.take(4)[0:1, :]
    t["thr_b"] = f32.take(4)
    t["nthr_b"] = f32.take(4)
    _ln_views(t, f32)
    t["gpart"] = t["ssq"][0][:, 0:160]
    t["row_a"] = t["row_m"]
    return t


def mk_t_gqa(shared, b16, f32):
    t = dict(shared)
    t["Kg"] = [b16.take(EXT) for _ in range(NKV)]
    t["Vg"] = [b16.take(EXT) for _ in range(NKV)]
    t["Qh"] = b16.take(NTOK)
    t["maskb"] = [b16.take(8 * 512) for _ in range(4)]
    t["probs"] = [b16.take(512), b16.take(512)]
    t["probm"] = [b16.take(512), b16.take(512)]
    t["lp"] = [b16.take(512), b16.take(512)]
    t["ctxo"] = [b16.take(512), b16.take(512)]
    UB = b16.take(24576)
    t["wkvb"] = [UB[:, g * 4096:(g + 1) * 4096] for g in range(4)]
    t["xbf_c"] = [UB[:, 16384:24576]]
    t["wqb"] = [UB[:, 0:2048], UB[:, 2048:4096]]
    t["cqb"] = [UB[:, 4096:12288], UB[:, 12288:20480]]
    t["ctxb_c"] = UB[:, 4096:12288]
    t["wob_c"] = [UB[:, 0:2048], UB[:, 2048:4096]]
    t["wq3p"] = b16.take(704)
    t["qa_p"] = b16.take(704)
    arena = f32.take(SEQ)
    t["arena"] = arena
    UF = f32.take(7680)
    t["rope_c"] = [UF[0:64, 2048 + i * 512: 2048 + (i + 1) * 512] for i in range(2)] + \
                  [UF[0:64, 3072 + i * 512: 3072 + (i + 1) * 512] for i in range(2)]
    t["linv"] = UF[0:1, 4608:5120]
    t["bcl"] = UF[:, 5120:5632]
    t["row_m"] = UF[0:1, 0:512]
    t["row_v"] = UF[0:1, 512:1024]
    t["row_r"] = UF[0:1, 1024:1536]
    t["row_r2"] = UF[0:1, 1536:2048]
    t["row_mr"] = UF[0:1, 2048:2560]
    t["ssq"] = [UF[:, 2560:3072], UF[:, 3072:3584]]
    t["xres"] = [UF[:, 3584:4096], UF[:, 4096:4608]]
    t["bc_r"] = UF[:, 5632:6144]
    t["bc_mr"] = UF[:, 6144:6656]
    t["tmp_ln"] = [UF[:, 6656:7168], UF[:, 7168:7680]]
    t["obuf"] = [UF[:, 4608:5120], UF[:, 5120:5632]]
    t["gbuf_pre"] = [UF[:, 5632:6656], UF[:, 6656:7680]]
    t["mq"] = f32.take(1)
    t["mk"] = f32.take(1)
    t["prod"] = f32.take(1)
    t["negc"] = f32.take(1)[0:1, :]
    t["negc2"] = f32.take(1)[0:1, :]
    t["negc_b"] = f32.take(1)
    t["xg32"] = UF[:, 6656:7168]
    return t


ALL_LANES = sorted(set(MLA_LANES) | set(FFN_LANES) | set(GQA_LANES)
                   | {"CCX", "LNB", "LG0", "LG1", "SW3P0", "GIN", "LGR", "CCG"})


def stage_barrier(p, c, scr):
    PE, ACs, DV = c["PE"], c["AC"], c["DV"]
    p.top("vector", "tensor_copy", scr[0][0:1, 0:8], scr[1][0:1, 0:8], inc=[DV.inc()])
    p.top("scalar", "activation", scr[0][0:1, 8:16], scr[1][0:1, 8:16], AF.Copy,
          inc=[c["AC"].inc()])
    waits = [PE.at(), DV.at(), ACs.at()]
    if "GP" in c and c["GP"].n:
        waits.append(c["GP"].at())
    for k in ALL_LANES:
        ln = c["lanes"][k]
        if ln.n:
            waits.append(ln.at())
    for eng in Prog.ENGS:
        for w in waits:
            p.wait(eng, *w)
    return waits


def build_fused():
    nc = bass.Bass("TRN2", target_bir_lowering=False, debug=False)
    io = {}
    io["xT"] = nc.dram_tensor("xT", [DM, SEQ], F32, kind="ExternalInput").ap()
    io["wdkv_in"] = nc.dram_tensor("wdkv_in", [DM, KVC], F32, kind="ExternalInput").ap()
    io["wdqT4"] = nc.dram_tensor("wdqT4", [12, 128, NDT * 128], F32, kind="ExternalInput").ap()
    io["wuqT4"] = nc.dram_tensor("wuqT4", [NH, 128, 12 * 128], F32, kind="ExternalInput").ap()
    io["wukvT4"] = nc.dram_tensor("wukvT4", [NH, 128, 8 * 128], F32, kind="ExternalInput").ap()
    io["wqr_in"] = nc.dram_tensor("wqr_in", [128, 64], F32, kind="ExternalInput").ap()
    io["wkr_in"] = nc.dram_tensor("wkr_in", [128, 64], F32, kind="ExternalInput").ap()
    io["woT4"] = nc.dram_tensor("woT4", [NDT, 128, 16 * 128], F32, kind="ExternalInput").ap()
    for nm, sh in [("bdkv_in", [128, 4]), ("bdq_in", [128, 12]), ("buk_in", [128, 16]),
                   ("buq_in", [128, 16]), ("bo_in", [128, 16]), ("buv_in", [1, DM]),
                   ("ropek_in", [64, SEQ]), ("ropeq_in", [64, NTOK]),
                   ("ones_col_in", [128, 1]), ("ones_row_in", [1, 128]),
                   ("consts_in", [1, 8])]:
        io[nm] = nc.dram_tensor(nm, sh, F32, kind="ExternalInput").ap()
    for L in (0, 1):
        io[f"w1L{L}"] = nc.dram_tensor(f"w1L{L}", [NHT, 128, NDT * 128], F32, kind="ExternalInput").ap()
        io[f"w2L{L}"] = nc.dram_tensor(f"w2L{L}", [NHT, 128, NDT * 128], F32, kind="ExternalInput").ap()
        io[f"w3L{L}"] = nc.dram_tensor(f"w3L{L}", [NDT, 128, NHT * 128], F32, kind="ExternalInput").ap()
    io["wqT4"] = nc.dram_tensor("wqT4", [NH, 128, NDT * 128], F32, kind="ExternalInput").ap()
    io["wkvT4"] = nc.dram_tensor("wkvT4", [NKV, 128, 2 * NDT * 128], F32, kind="ExternalInput").ap()
    io["woT4c"] = nc.dram_tensor("woT4c", [NDT, 128, 16 * 128], F32, kind="ExternalInput").ap()
    io["ropek_c"] = nc.dram_tensor("ropek_c", [128, EXT], F32, kind="ExternalInput").ap()
    io["ropeq_c"] = nc.dram_tensor("ropeq_c", [128, NTOK], F32, kind="ExternalInput").ap()
    io["masks"] = nc.dram_tensor("masks", [4, 128, 8 * 512], BF16, kind="ExternalInput").ap()
    io["nbr_idx"] = nc.dram_tensor("nbr_idx", [128, 16], mybir.dt.int32, kind="ExternalInput").ap()
    io["gslabL0"] = nc.dram_tensor("gslabL0", [128, 33 * 1024], F32, kind="ExternalInput").ap()
    io["gslabL1"] = nc.dram_tensor("gslabL1", [128, 33 * 1024], F32, kind="ExternalInput").ap()
    io["out"] = nc.dram_tensor("outT", [DM, NTOK], F32, kind="ExternalOutput").ap()
    g_in0 = nc.dram_tensor("g_in0", [1, 4], F32, kind="Internal")
    g_out0 = nc.dram_tensor("g_out0", [1, 4], F32, kind="Internal")
    g_in1 = nc.dram_tensor("g_in1", [1, 4], F32, kind="Internal")
    g_out1 = nc.dram_tensor("g_out1", [1, 4], F32, kind="Internal")
    r1_s = nc.dram_tensor("r1_s", [DM, NTOK], F32, kind="Internal").ap()
    r2_s = nc.dram_tensor("r2_s", [DM, NTOK], F32, kind="Internal").ap()
    r3_s = nc.dram_tensor("r3_s", [DM, NTOK], F32, kind="Internal").ap()
    cin = nc.dram_tensor("cin", [DM, 512], F32, kind="Internal")
    cout = nc.dram_tensor("cout", [8 * DM, 512], F32, kind="Internal")
    scr_dram = {
        "ctx_s": nc.dram_tensor("ctx_s", [DM, NTOK], BF16, kind="Internal").ap(),
        "cq_s": nc.dram_tensor("cq_s", [12, 128, NTOK], BF16, kind="Internal").ap(),
        "r2b_s": nc.dram_tensor("r2b_s", [4, 128, NDT * 512], BF16, kind="Internal").ap(),
        "w1q_s": nc.dram_tensor("w1q_s", [NHT, 128, NDT * 128], BF16, kind="Internal").ap(),
        "w2q_s": nc.dram_tensor("w2q_s", [NHT, 128, NDT * 128], BF16, kind="Internal").ap(),
        "w3q_s": nc.dram_tensor("w3q_s", [NDT, 128, NHT * 128], BF16, kind="Internal").ap(),
    }
    with ExitStack() as es:
        sb = lambda name, shape, dt: es.enter_context(nc.sbuf_tensor(name, shape, dt))
        shared = {}
        shared["one_col"] = sb("one_col", [128, 1], F32)
        shared["one_row"] = sb("one_row", [1, 128], F32)
        shared["one_col16"] = sb("one_col16", [128, 1], BF16)
        shared["one_row16"] = sb("one_row16", [1, 128], BF16)
        shared["crow"] = sb("crow", [1, 8], F32)
        scr0 = sb("scr0", [1, 16], F32)
        scr1 = sb("scr1", [1, 16], F32)
        shared["thrB0"] = sb("thrB0", [128, 4], F32)
        shared["nthrB0"] = sb("nthrB0", [128, 4], F32)
        shared["thrB1"] = sb("thrB1", [128, 4], F32)
        shared["nthrB1"] = sb("nthrB1", [128, 4], F32)
        shared["gpartP"] = sb("gpartP", [128, 308], F32)
        sidx = sb("sidx", [128, 16], mybir.dt.int32)
        GB16 = sb("GB16", [128, 72704], BF16)
        GF32 = sb("GF32", [128, 15960], F32)
        P, PL, P7 = _mk_ps(nc, es)
        c = {k: Ctr(es, nc, k) for k in ("PE", "AC", "DV", "GP")}
        c["lanes"] = {k: Ctr(es, nc, k) for k in ALL_LANES}
        p = Prog(nc)
        p.chains = {e: Ctr(es, nc, "CH_" + e) for e in ("vector", "scalar", "tensor", "gpsimd")}
        M = c["lanes"]["MISC"]

        def mload(dst, src, eng="gpsimd"):
            p.op(eng, "dma_start", dst, src, inc=[M.inc(16)])

        b16 = Slab(GB16, 72704)
        f32 = Slab(GF32, 15960)

        t = mk_t_mla(shared, b16, f32)
        t.update(scr_dram)
        _ps_attn(t, P, PL, P7)
        mload(shared["one_col"][:, :], io["ones_col_in"], "sync")
        mload(shared["one_row"][:, :], io["ones_row_in"], "sync")
        mload(shared["crow"][:, :], io["consts_in"], "sync")
        mload(shared["one_col16"][:, :], io["ones_col_in"])
        mload(shared["one_row16"][:, :], io["ones_row_in"])
        mload(sidx[:, :], io["nbr_idx"], "sync")
        p.op("vector", "memset", scr1[:, :], 1.0, inc=[c["DV"].inc()])
        for dt in range(NDT):
            mload(t["wdkv"][:, dt * 512:(dt + 1) * 512],
                  io["wdkv_in"][dt * 128:(dt + 1) * 128, :])
        mload(t["wqr"][:, :], io["wqr_in"])
        mload(t["wkr"][:, :], io["wkr_in"])
        for nm in ("bdkv", "bdq", "buk", "buq", "bo"):
            mload(t[nm][:, :], io[nm + "_in"], "sync")
        mload(t["arena"][0:64, :], io["ropek_in"], "sync")
        mload(t["qcos"], io["ropeq_in"][0:32, :], "sync")
        mload(t["qsin"], io["ropeq_in"][32:64, :], "sync")
        io_a = dict(io)
        io_a["out"] = r1_s
        g0 = GammaPre(nc, p, c, t, io["gslabL0"], io["w3L0"],
                      shared["thrB0"], shared["nthrB0"], scr_dram["w3q_s"],
                      g_in0, g_out0)
        g0.step(6, 0)
        emit_mla(nc, p, c, t, io_a,
                 head_hook=lambda h: g0.step(1, 3),
                 qn_hook=lambda: g0.step(2, 3))
        pre0 = g0.finish()
        stage_barrier(p, c, (scr0, scr1))

        b16.reset(); f32.reset()
        t = mk_t_ffn(shared, b16, f32)
        t.update(scr_dram)
        _ps_ffn(t, P, PL, P7)
        emit_ffn(nc, p, c, t, r1_s, io["w1L0"], io["w2L0"], io["w3L0"], r2_s, True,
                 pre=pre0)
        stage_barrier(p, c, (scr0, scr1))

        b16.reset(); f32.reset()
        t = mk_t_gqa(shared, b16, f32)
        t.update(scr_dram)
        _ps_attn(t, P, PL, P7)
        g1 = GammaPre(nc, p, c, t, io["gslabL1"], io["w3L1"],
                      shared["thrB1"], shared["nthrB1"], scr_dram["w3q_s"],
                      g_in1, g_out1)
        g1.step(6, 0)

        CC = c["lanes"]["CCX"]
        p.op("gpsimd", "dma_start", cin.ap()[:, :], r2_s[:, NTOK - 512:NTOK],
             inc=[CC.inc(16)])
        p.wait("gpsimd", *CC.at())
        p.op("gpsimd", "collective_compute", "AllGather", ALU.bypass,
             replica_groups=[list(range(8))], ins=[cin.ap().opt()],
             outs=[cout.ap().opt()], inc=[CC.inc(1)])
        p.wait("gpsimd", *CC.at())

        mload(t["arena"][0:128, 0:EXT], io["ropek_c"], "sync")
        mload(t["arena"][0:128, 2560:4608], io["ropeq_c"], "sync")
        io_c = dict(io)
        io_c["r2T"] = r2_s
        io_c["woT4"] = io["woT4c"]
        io_c["out"] = r3_s
        io_c["nbr"] = (cout.ap(), sidx, t["xg32"])
        emit_gqa(nc, p, c, t, io_c,
                 head_hook=lambda h: g1.step(1, 3),
                 qn_hook=lambda: g1.step(2, 3))
        pre1 = g1.finish()
        stage_barrier(p, c, (scr0, scr1))

        b16.reset(); f32.reset()
        t = mk_t_ffn(shared, b16, f32)
        t.update(scr_dram)
        _ps_ffn(t, P, PL, P7)
        emit_ffn(nc, p, c, t, r3_s, io["w1L1"], io["w2L1"], io["w3L1"], io["out"], True,
                 pre=pre1)
        for k in ALL_LANES:
            if k.startswith("S") and c["lanes"][k].n:
                p.wait("sync", *c["lanes"][k].at())
        p.finish()
    return nc



def tile_w12(w):
    wT = np.ascontiguousarray(w.T)
    x = wT.reshape(NDT, 128, NHT, 128).transpose(2, 1, 0, 3)
    return np.ascontiguousarray(x.reshape(NHT, 128, NDT * 128))


def tile_w3(w3):
    wT = np.ascontiguousarray(w3.T)
    x = wT.reshape(NHT, 128, NDT, 128).transpose(2, 1, 0, 3)
    return np.ascontiguousarray(x.reshape(NDT, 128, NHT * 128))



def _mla_host_prep(ins):
    d = {}
    d["wdkv_in"] = ins["L0_attn_W_DKV"]
    d["wdqT4"] = np.ascontiguousarray(
        ins["L0_attn_W_DQ"].reshape(16, 128, 12, 128).transpose(2, 1, 0, 3).reshape(12, 128, 2048))
    d["wuqT4"] = np.ascontiguousarray(
        ins["L0_attn_W_UQ"].reshape(12, 128, 16, 128).transpose(2, 1, 0, 3).reshape(16, 128, 12 * 128))
    uk = ins["L0_attn_W_UK"].reshape(4, 128, 16, 128).transpose(2, 1, 0, 3).reshape(16, 128, 512)
    uv = ins["L0_attn_W_UV"].reshape(4, 128, 16, 128).transpose(2, 1, 0, 3).reshape(16, 128, 512)
    d["wukvT4"] = np.ascontiguousarray(np.concatenate([uk, uv], axis=2))
    d["wqr_in"] = ins["L0_attn_W_QR"]
    d["wkr_in"] = ins["L0_attn_W_KR"]
    d["woT4"] = np.ascontiguousarray(
        ins["L0_attn_W_O"].reshape(16, 128, 16, 128).transpose(2, 1, 0, 3).reshape(16, 128, 16 * 128))
    d["bdkv_in"] = np.ascontiguousarray(ins["L0_attn_b_DKV"].reshape(4, 128).T)
    d["bdq_in"] = np.ascontiguousarray(ins["L0_attn_b_DQ"].reshape(12, 128).T)
    d["buk_in"] = np.ascontiguousarray(ins["L0_attn_b_UK"].reshape(16, 128).T)
    d["buq_in"] = np.ascontiguousarray(ins["L0_attn_b_UQ"].reshape(16, 128).T)
    d["bo_in"] = np.ascontiguousarray(ins["L0_attn_b_O"].reshape(16, 128).T)
    d["buv_in"] = np.ascontiguousarray(ins["L0_attn_b_UV"].reshape(1, 2048))
    d["ones_col_in"] = np.ones((128, 1), np.float32)
    d["ones_row_in"] = np.ones((1, 128), np.float32)
    d["consts_in"] = np.array([[EPS_LN, 0, 0, 0, 0, 0, 0, 0]], np.float32)
    return d


def _rope_tables_mla(pos):
    inv = (1.0 / 10000.0 ** (np.arange(0, 64, 2, dtype=np.float32) / 64.0)).astype(np.float32)
    fr = pos[None, :].astype(np.float32) * inv[:, None]
    return np.ascontiguousarray(np.concatenate([np.cos(fr), np.sin(fr)], axis=0)).astype(np.float32)


def _rope_tables_gqa(pos):
    inv = (1.0 / 10000.0 ** (np.arange(0, 128, 2, dtype=np.float32) / 128.0)).astype(np.float32)
    ang = pos[None, :].astype(np.float32) * inv[:, None]
    return np.ascontiguousarray(np.concatenate([np.cos(ang), np.sin(ang)], 0)).astype(np.float32)


def _gqa_host_weights(ins):
    d = {}
    d["wqT4"] = np.ascontiguousarray(
        ins["L1_attn_Wq"].T.reshape(16, 128, 16, 128).transpose(2, 1, 0, 3).reshape(16, 128, 2048))
    kk = ins["L1_attn_Wk"].T.reshape(16, 128, 4, 128).transpose(2, 1, 0, 3).reshape(4, 128, 2048)
    vv = ins["L1_attn_Wv"].T.reshape(16, 128, 4, 128).transpose(2, 1, 0, 3).reshape(4, 128, 2048)
    d["wkvT4"] = np.ascontiguousarray(np.concatenate([kk, vv], axis=2))
    d["woT4"] = np.ascontiguousarray(
        ins["L1_attn_Wo"].T.reshape(16, 128, 16, 128).transpose(2, 1, 0, 3).reshape(16, 128, 2048))
    d["ones_col_in"] = np.ones((128, 1), np.float32)
    d["ones_row_in"] = np.ones((1, 128), np.float32)
    d["consts_in"] = np.array([[EPS_LN, 0, 0, 0, 0, 0, 0, 0]], np.float32)
    return d


def _gqa_masks(qi):
    own0 = qi * NTOK
    j_idx = np.arange(8)
    p_idx = np.arange(128)
    c_idx = np.arange(512)
    m = np.zeros((4, 128, 8 * 512), dtype=np.float32)
    for qb in range(4):
        kg = own0 - 512 + (4 * qb + j_idx)[:, None] * 128 + p_idx[None, :]
        qg = own0 + qb * 512 + c_idx
        valid = ((kg[:, :, None] <= qg[None, None, :])
                 & (qg[None, None, :] - kg[:, :, None] < 512)
                 & (kg[:, :, None] >= 0))
        m[qb] = valid.transpose(1, 0, 2).reshape(128, 8 * 512)
    return m.astype(BF)


def _gamma_slab(w1, w2, w3, core):
    parts = []
    for w in (w1, w2, w3):
        flat = w.reshape(-1)
        n = flat.size // NCORES
        parts.append(flat[core * n:(core + 1) * n].reshape(128, 11264))
    return np.ascontiguousarray(np.concatenate(parts, axis=1))


def _ffn_host(w1, w2, w3):
    return {"w1L": tile_w12(w1), "w2L": tile_w12(w2), "w3L": tile_w3(w3),
            "ones_col_in": np.ones((128, 1), np.float32),
            "ones_row_in": np.ones((1, 128), np.float32),
            "consts_in": np.array([[EPS_LN, 0, 0, 0, 0, 0, 0, 0]], np.float32)}


def kernel(**inputs):
    x = inputs["L0_attn_hidden_states"]
    xT = [np.ascontiguousarray(x[b].T) for b in range(2)]

    nc = build_fused()
    base = _mla_host_prep(inputs)
    base.update(_ffn_host(inputs["L0_ffn_w1"], inputs["L0_ffn_w2"], inputs["L0_ffn_w3"]))
    base["w1L0"], base["w2L0"], base["w3L0"] = base.pop("w1L"), base.pop("w2L"), base.pop("w3L")
    d1 = _ffn_host(inputs["L1_ffn_w1"], inputs["L1_ffn_w2"], inputs["L1_ffn_w3"])
    base["w1L1"], base["w2L1"], base["w3L1"] = d1["w1L"], d1["w2L"], d1["w3L"]
    gq = _gqa_host_weights(inputs)
    base["wqT4"], base["wkvT4"], base["woT4c"] = gq["wqT4"], gq["wkvT4"], gq["woT4"]
    in_maps = []
    for c in range(NCORES):
        b, qi = c // 4, c % 4
        im = dict(base)
        xb = xT[b]
        im["xT"] = np.ascontiguousarray(
            np.concatenate([xb[:, qi * NTOK:], xb[:, :qi * NTOK]], axis=1))
        pos_rot = np.concatenate([np.arange(qi * NTOK, SEQ), np.arange(0, qi * NTOK)])
        im["ropek_in"] = _rope_tables_mla(pos_rot)
        im["ropeq_in"] = _rope_tables_mla(np.arange(qi * NTOK, (qi + 1) * NTOK))
        own0 = qi * NTOK
        im["ropek_c"] = _rope_tables_gqa(np.maximum(np.arange(own0 - 512, own0 + NTOK), 0))
        im["ropeq_c"] = _rope_tables_gqa(np.arange(own0, own0 + NTOK))
        im["masks"] = _gqa_masks(qi)
        nb = (c - 1) % 8
        im["nbr_idx"] = np.ascontiguousarray(
            (nb * DM + np.arange(DM)).astype(np.int32).reshape(16, 128).T)
        im["gslabL0"] = _gamma_slab(inputs["L0_ffn_w1"], inputs["L0_ffn_w2"],
                                    inputs["L0_ffn_w3"], c)
        im["gslabL1"] = _gamma_slab(inputs["L1_ffn_w1"], inputs["L1_ffn_w2"],
                                    inputs["L1_ffn_w3"], c)
        in_maps.append(im)
    res = run_bass_kernel_spmd(nc, in_maps, core_ids=list(range(NCORES)))
    out = np.empty((2, SEQ, DM), np.float32)
    for c in range(NCORES):
        b, qi = c // 4, c % 4
        out[b, qi * NTOK:(qi + 1) * NTOK, :] = np.asarray(res.results[c]["outT"]).T
    return out


def kernel_unfused(**inputs):
    x = inputs["L0_attn_hidden_states"]
    xT = [np.ascontiguousarray(x[b].T) for b in range(2)]

    nc_a = build_mla_stage()
    base_a = _mla_host_prep(inputs)
    in_maps = []
    for c in range(NCORES):
        b, qi = c // 4, c % 4
        im = dict(base_a)
        xb = xT[b]
        im["xT"] = np.ascontiguousarray(
            np.concatenate([xb[:, qi * NTOK:], xb[:, :qi * NTOK]], axis=1))
        pos_rot = np.concatenate([np.arange(qi * NTOK, SEQ), np.arange(0, qi * NTOK)])
        im["ropek_in"] = _rope_tables_mla(pos_rot)
        im["ropeq_in"] = _rope_tables_mla(np.arange(qi * NTOK, (qi + 1) * NTOK))
        in_maps.append(im)
    res = run_bass_kernel_spmd(nc_a, in_maps, core_ids=list(range(NCORES)))
    r1 = [np.asarray(res.results[c]["r1T"]) for c in range(NCORES)]

    nc_f = build_ffn_stage(out_f32=True)
    base_b = _ffn_host(inputs["L0_ffn_w1"], inputs["L0_ffn_w2"], inputs["L0_ffn_w3"])
    in_maps = [dict(base_b, rT_in=r1[c]) for c in range(NCORES)]
    res = run_bass_kernel_spmd(nc_f, in_maps, core_ids=list(range(NCORES)))
    r2 = [np.asarray(res.results[c]["rT_out"]) for c in range(NCORES)]

    nc_c = build_gqa_stage()
    base_c = _gqa_host_weights(inputs)
    in_maps = []
    for c in range(NCORES):
        b, qi = c // 4, c % 4
        im = dict(base_c)
        im["r2T"] = r2[c]
        if qi == 0:
            im["r2nT"] = np.zeros((DM, 512), np.float32)
        else:
            im["r2nT"] = np.ascontiguousarray(r2[c - 1][:, NTOK - 512:])
        own0 = qi * NTOK
        im["ropek_c"] = _rope_tables_gqa(np.maximum(np.arange(own0 - 512, own0 + NTOK), 0))
        im["ropeq_c"] = _rope_tables_gqa(np.arange(own0, own0 + NTOK))
        im["masks"] = _gqa_masks(qi)
        in_maps.append(im)
    res = run_bass_kernel_spmd(nc_c, in_maps, core_ids=list(range(NCORES)))
    r3 = [np.asarray(res.results[c]["r3T"]) for c in range(NCORES)]

    base_d = _ffn_host(inputs["L1_ffn_w1"], inputs["L1_ffn_w2"], inputs["L1_ffn_w3"])
    in_maps = [dict(base_d, rT_in=r3[c]) for c in range(NCORES)]
    res = run_bass_kernel_spmd(nc_f, in_maps, core_ids=list(range(NCORES)))

    out = np.empty((2, SEQ, DM), np.float32)
    for c in range(NCORES):
        b, qi = c // 4, c % 4
        out[b, qi * NTOK:(qi + 1) * NTOK, :] = np.asarray(res.results[c]["rT_out"]).T
    return out
